# version 01
# baseline (speedup 1.0000x reference)
import numpy as np
import ml_dtypes
import concourse.bass as bass
import concourse.mybir as mybir
from contextlib import ExitStack
from concourse.bass_utils import run_bass_kernel_spmd

F32 = mybir.dt.float32
BF16 = mybir.dt.bfloat16
AF = mybir.ActivationFunctionType
ALU = mybir.AluOpType
AX = mybir.AxisListType
NEG = -1.0e30

class Prog:
    def __init__(self):
        self.nc = bass.Bass("TRN2", target_bir_lowering=False, debug=False)
        self.es = ExitStack()
        self.ctr = {}

    def sb(self, name, shape, dt=F32):
        return self.es.enter_context(self.nc.sbuf_tensor(name, shape, dt))

    def ps(self, name, shape, dt=F32):
        return self.es.enter_context(self.nc.psum_tensor(name, shape, dt))

    def sem(self, name):
        s = self.es.enter_context(self.nc.semaphore(name))
        self.ctr[name] = 0
        return s

    def block(self):
        return self.es.enter_context(self.nc.Block())

    def inc(self, sem, n=1):
        self.ctr[sem.name] += n
        return self.ctr[sem.name]

    def val(self, sem):
        return self.ctr[sem.name]

    def scrapv(self):
        if not hasattr(self, "_scrapv"):
            self._scrapv = self.sb("_scrapv", [128, 1], F32)
        return self._scrapv

    def close(self):
        self.es.close()
        return self.nc


def run_stream(p, blk, sems, tasks, psums, stg=None, nslots=4, slot_w=512,
               t_waits=(), v_waits=()):
    mm, ev, so = sems["mm"], sems["ev"], sems["so"]
    NB = len(psums)
    ev0 = p.val(ev)
    so0 = p.val(so)
    mm_vals = []
    ev_vals = []
    n_stores = sum(1 for t in tasks if t.get("store") is not None)

    @blk.tensor
    def _(t):
        t.wait_ge(ev, ev0)
        for s_, v_ in t_waits:
            t.wait_ge(s_, v_)
        for i, task in enumerate(tasks):
            if i >= NB:
                t.wait_ge(ev, ev0 + (i - NB) + 1)
            task["mm"](t, psums[i % NB]).then_inc(mm, 1)
            mm_vals.append(p.inc(mm))

    store_idx = [0]
    slot_of_task = {}

    @blk.vector
    def _(v):
        for s_, v_ in v_waits:
            v.wait_ge(s_, v_)
        for i, task in enumerate(tasks):
            v.wait_ge(mm, mm_vals[i])
            if task.get("store") is not None:
                si = store_idx[0]
                if si >= nslots:
                    v.wait_ge(so, so0 + 16 * si)
                slot = stg[:, (si % nslots) * slot_w:(si % nslots) * slot_w + task.get("dt", slot_w)]
                slot_of_task[i] = slot
                store_idx[0] += 1
                task["evac"](v, psums[i % NB], slot)
                v_last = v.tensor_copy(out=p.scrapv()[:], in_=p.scrapv()[:])
            else:
                v_last = task["evac"](v, psums[i % NB], None)
            v_last.then_inc(ev, 1)
            ev_vals.append(p.inc(ev))

    if n_stores:
        @blk.sync
        def _(s):
            for i, task in enumerate(tasks):
                if task.get("store") is None:
                    continue
                s.wait_ge(ev, ev_vals[i])
                s.dma_start(out=task["store"], in_=slot_of_task[i]).then_inc(so, 16)
                p.inc(so, 16)

    return {"mm_vals": mm_vals, "ev_vals": ev_vals}


class Sched:

    ENGS = ("t", "v", "s", "g", "y")

    def __init__(self, p, blk):
        self.p = p
        self.blk = blk
        self.sem = {e: p.sem("S" + e) for e in self.ENGS}
        self.reg = {}
        self.waited = {e: {e2: 0 for e2 in self.ENGS} for e in self.ENGS}
        self.scrap = {e: p.sb(f"scrap_{e}", [128, 1], F32) for e in ("v", "s")}
        self.flush_at = {"v": 0, "s": 0}

    def _overlaps(self, name, lo, hi):
        return [seg for seg in self.reg.get(name, []) if seg[0] < hi and lo < seg[1]]

    def emit(self, eng, fn, reads=(), writes=(), inc=1, small=False, extra_waits=()):
        p = self.p
        need = {}
        dma_engs = ("y", "g")
        for (name, lo, hi) in reads:
            for seg in self._overlaps(name, lo, hi):
                w = seg[2]
                if w is not None:
                    we, wv = w
                    if we != eng or seg[4] or eng in dma_engs or we in dma_engs:
                        need[we] = max(need.get(we, 0), wv)
        for (name, lo, hi) in writes:
            for seg in self._overlaps(name, lo, hi):
                w = seg[2]
                if w is not None:
                    we, wv = w
                    if we != eng or eng in dma_engs or we in dma_engs:
                        need[we] = max(need.get(we, 0), wv)
                for (re_, rv) in seg[3]:
                    if re_ != eng or eng in dma_engs or re_ in dma_engs:
                        need[re_] = max(need.get(re_, 0), rv)
        for (se, sv) in extra_waits:
            need[se] = max(need.get(se, 0), sv)
        if eng in dma_engs:
            for we in ("v", "s"):
                v = need.get(we, 0)
                if v > 0 and self.flush_at[we] < v:
                    scr = self.scrap[we]
                    if we == "v":
                        def _fl(E, scr=scr):
                            return E.tensor_copy(out=scr[:], in_=scr[:])
                        blkd = self.blk.vector
                    else:
                        def _fl(E, scr=scr):
                            return E.copy(out=scr[:], in_=scr[:])
                        blkd = self.blk.scalar
                    semw = self.sem[we]

                    def body_fl(E, _fl=_fl, semw=semw):
                        _fl(E).then_inc(semw, 1)
                    blkd(body_fl)
                    self.flush_at[we] = p.inc(semw)
                if v > 0:
                    need[we] = self.flush_at[we]
        for e2 in ("y", "g"):
            if need.get(e2, 0) > 0:
                need[e2] = p.val(self.sem[e2])
        todo = [(e2, v) for e2, v in need.items()
                if v > self.waited[eng].get(e2, 0)]
        sem = self.sem
        blkdec = getattr(self.blk, {"t": "tensor", "v": "vector", "s": "scalar",
                                    "g": "gpsimd", "y": "sync"}[eng])

        def body(E):
            for e2, v in todo:
                E.wait_ge(sem[e2], v)
            last = fn(E)
            last.then_inc(sem[eng], inc)
        blkdec(body)
        for e2, v in todo:
            self.waited[eng][e2] = max(self.waited[eng].get(e2, 0), v)
        p.inc(sem[eng], inc)
        myval = p.val(sem[eng])
        for (name, lo, hi) in writes:
            segs = self.reg.setdefault(name, [])
            newsegs = []
            for seg in segs:
                if seg[0] < hi and lo < seg[1]:
                    if seg[0] < lo:
                        newsegs.append([seg[0], lo, seg[2], list(seg[3]), seg[4]])
                    if hi < seg[1]:
                        newsegs.append([hi, seg[1], seg[2], list(seg[3]), seg[4]])
                else:
                    newsegs.append(seg)
            newsegs.append([lo, hi, (eng, myval), [], small])
            self.reg[name] = newsegs
        for (name, lo, hi) in reads:
            for seg in self._overlaps(name, lo, hi):
                seg[3].append((eng, myval))
        return myval

    def barrier(self):
        p = self.p
        vals = {e: p.val(self.sem[e]) for e in self.ENGS}
        for e in self.ENGS:
            blkdec = getattr(self.blk, {"t": "tensor", "v": "vector", "s": "scalar",
                                        "g": "gpsimd", "y": "sync"}[e])
            sem = self.sem

            def body(E, e=e):
                for e2 in self.ENGS:
                    if e2 != e and vals[e2] > self.waited[e].get(e2, 0):
                        E.wait_ge(sem[e2], vals[e2])
            blkdec(body)
            for e2 in self.ENGS:
                if e2 != e:
                    self.waited[e][e2] = max(self.waited[e].get(e2, 0), vals[e2])



def _ns_A():
    pass

D = 1024
TOWN = 2048
TFULL = 4096
NI = TOWN // 128


def build_A1():
    p = Prog()
    nc = p.nc
    xo = nc.dram_tensor("xo", [D, TOWN], F32, kind="ExternalInput").ap()
    W = {n: nc.dram_tensor(n, [D, D], F32, kind="ExternalInput").ap()
         for n in ("Wq", "Wk", "Wv")}
    bqh = nc.dram_tensor("bq", [128, 8], F32, kind="ExternalInput").ap()
    bkh = nc.dram_tensor("bk", [128, 8], F32, kind="ExternalInput").ap()
    bvr = nc.dram_tensor("bv", [1, D], F32, kind="ExternalInput").ap()
    qo = nc.dram_tensor("qo", [D, TOWN], BF16, kind="ExternalOutput").ap()
    ko = nc.dram_tensor("ko", [D, TOWN], BF16, kind="ExternalOutput").ap()
    vTo = nc.dram_tensor("vTo", [TOWN, D], BF16, kind="ExternalOutput").ap()
    dst = {"q": qo, "k": ko}

    xb = p.sb("xb", [128, 8 * TOWN], BF16)
    ws = [p.sb(f"w{i}", [128, 8 * D], BF16) for i in range(3)]
    bqt = p.sb("bqt", [128, 8], F32)
    bkt = p.sb("bkt", [128, 8], F32)
    bvrow = p.sb("bvrow", [1, D], F32)
    bvB = p.sb("bvB", [128, D], F32)
    ones1 = p.sb("ones1", [1, 128], F32)
    stg = p.sb("stg", [128, 4 * 512], BF16)
    ring = [p.ps(f"pm{i}", [128, 512], F32) for i in range(4)]

    ld, lf, mm, ev, so = (p.sem(n) for n in ("ld", "lf", "mm", "ev", "so"))
    sems = {"mm": mm, "ev": ev, "so": so}
    blk = p.block()

    @blk.gpsimd
    def _(g):
        for c in range(8):
            g.dma_start(out=xb[:, c * TOWN:(c + 1) * TOWN],
                        in_=xo[c * 128:(c + 1) * 128, :]).then_inc(ld, 16)
            p.inc(ld, 16)
        for wi, wn in enumerate(("Wq", "Wk", "Wv")):
            for c in range(8):
                g.dma_start(out=ws[wi][:, c * D:(c + 1) * D],
                            in_=W[wn][c * 128:(c + 1) * 128, :]).then_inc(ld, 16)
                p.inc(ld, 16)
    ld_all = p.val(ld)

    @blk.sync
    def _(s):
        s.dma_start(out=bqt[:], in_=bqh[:]).then_inc(lf, 16)
        s.dma_start(out=bkt[:], in_=bkh[:]).then_inc(lf, 16)
        s.dma_start(out=bvrow[:], in_=bvr[:]).then_inc(lf, 16)
        p.inc(lf, 48)
    lf_all = p.val(lf)

    @blk.vector
    def _(v):
        v.wait_ge(lf, lf_all)
        v.memset(ones1[:], 1.0).then_inc(ev, 1)
        p.inc(ev)
    ev_pre = p.val(ev)

    tasks = []

    def bcast_task(half):
        def mmf(t, ps_):
            return t.matmul(ps_[:, 0:512], ones1[0:1, :],
                            bvrow[0:1, half * 512:(half + 1) * 512], start=True, stop=True)

        def evf(v, ps_, out):
            return v.tensor_copy(out=bvB[:, half * 512:(half + 1) * 512], in_=ps_[:, 0:512])
        return {"mm": mmf, "evac": evf}

    def proj_task(wi, fc, tc, name, bias_col, scale):
        def mmf(t, ps_):
            r = None
            for kc in range(8):
                r = t.matmul(ps_[:, 0:512],
                             ws[wi][:, kc * D + fc * 128: kc * D + fc * 128 + 128],
                             xb[:, kc * TOWN + tc * 512: kc * TOWN + tc * 512 + 512],
                             start=(kc == 0), stop=(kc == 7))
            return r

        def evf(v, ps_, out):
            if scale is None:
                return v.tensor_scalar(out=out, in0=ps_[:, 0:512], scalar1=bias_col,
                                       scalar2=None, op0=ALU.add)
            return v.tensor_scalar(out=out, in0=ps_[:, 0:512], scalar1=bias_col,
                                   scalar2=scale, op0=ALU.add, op1=ALU.mult)
        return {"mm": mmf, "evac": evf,
                "store": dst[name][fc * 128:(fc + 1) * 128, tc * 512:(tc + 1) * 512]}

    def vt_task(jb, dvc):
        def mmf(t, ps_):
            r = None
            for kc in range(8):
                r = t.matmul(ps_[:, 0:512],
                             xb[:, kc * TOWN + jb * 128: kc * TOWN + jb * 128 + 128],
                             ws[2][:, kc * D + dvc * 512: kc * D + dvc * 512 + 512],
                             start=(kc == 0), stop=(kc == 7))
            return r

        def evf(v, ps_, out):
            return v.scalar_tensor_tensor(out=out, in0=ps_[:, 0:512], scalar=1.0,
                                          in1=bvB[:, dvc * 512:(dvc + 1) * 512],
                                          op0=ALU.mult, op1=ALU.add)
        return {"mm": mmf, "evac": evf,
                "store": vTo[jb * 128:(jb + 1) * 128, dvc * 512:(dvc + 1) * 512]}

    tasks.append(bcast_task(0))
    tasks.append(bcast_task(1))
    for fc in range(8):
        for tc in range(4):
            tasks.append(proj_task(0, fc, tc, "q", bqt[:, fc:fc + 1], 1.0 / 32.0))
    for fc in range(8):
        for tc in range(4):
            tasks.append(proj_task(1, fc, tc, "k", bkt[:, fc:fc + 1], None))
    for jb in range(16):
        for dvc in range(2):
            tasks.append(vt_task(jb, dvc))

    run_stream(p, blk, sems, tasks, ring, stg=stg, nslots=4, slot_w=512,
               t_waits=[(ld, ld_all), (ev, ev_pre)],
               v_waits=[(lf, lf_all)])

    @blk.sync
    def _(s):
        s.wait_ge(so, p.val(so))

    p.close()
    return nc


def build_A2():
    p = Prog()
    nc = p.nc
    qf = nc.dram_tensor("qf", [D, TOWN], BF16, kind="ExternalInput").ap()
    kF = nc.dram_tensor("kF", [D, TFULL], BF16, kind="ExternalInput").ap()
    vTF = nc.dram_tensor("vTF", [TFULL, D], BF16, kind="ExternalInput").ap()
    Wo = nc.dram_tensor("Wo", [D, D], F32, kind="ExternalInput").ap()
    boh = nc.dram_tensor("bo", [128, 8], F32, kind="ExternalInput").ap()
    xo = nc.dram_tensor("xo", [D, TOWN], F32, kind="ExternalInput").ap()
    gixh = nc.dram_tensor("gix", [128, 16], F32, kind="ExternalInput").ap()
    rO = nc.dram_tensor("r", [D, TOWN], F32, kind="ExternalOutput").ap()

    kf = p.sb("kf", [128, 8 * TFULL], BF16)
    vT = p.sb("vT", [128, 32 * D], BF16)
    wo = p.sb("wo", [128, 8 * D], BF16)
    qc = p.sb("qc", [128, 2 * 8 * 128], BF16)
    xr = p.sb("xr", [128, 2 * 8 * 128], F32)
    Ssb = p.sb("Ssb", [128, TFULL], BF16)
    wsb = p.sb("wsb", [128, TFULL], BF16)
    wTt = p.sb("wTt", [128, TFULL], BF16)
    ctxs = p.sb("ctxs", [128, 8 * 128], BF16)
    msk = p.sb("msk", [128, 512], F32)
    jio = p.sb("jio", [128, 512], F32)
    jbase = p.sb("jbase", [128, 8], F32)
    gixa = p.sb("gixa", [128, 8], F32)
    bot = p.sb("bot", [128, 8], F32)
    gixt = p.sb("gixt", [128, 16], F32)
    mx = p.sb("mx", [128, 4], F32)
    nmxb = p.sb("nmxb", [128, 1], BF16)
    identb = p.sb("identb", [128, 128], BF16)
    stg = p.sb("stg", [128, 4 * 128], F32)

    ring = [p.ps(f"pm{i}", [128, 512], F32) for i in range(4)]
    pT = [p.ps(f"pT{i}", [128, 1024], BF16) for i in range(2)]

    ld, lf, mm, ev, sx, so = (p.sem(n) for n in ("ld", "lf", "mm", "ev", "sx", "so"))
    sems = {"mm": mm, "ev": ev, "so": so}
    blk = p.block()

    @blk.gpsimd
    def _(g):
        for c in range(8):
            g.dma_start(out=wo[:, c * D:(c + 1) * D],
                        in_=Wo[c * 128:(c + 1) * 128, :]).then_inc(ld, 16)
            p.inc(ld, 16)
        g.iota(jio[:], pattern=[[1, 512]], base=0, channel_multiplier=0,
               allow_small_or_imprecise_dtypes=True)
        g.iota(jbase[:], pattern=[[512, 8]], base=0, channel_multiplier=0,
               allow_small_or_imprecise_dtypes=True)
        g.memset(identb[:], 1.0)
        g.affine_select(identb[:], identb[:], pattern=[[-1, 128]], compare_op=ALU.is_equal,
                        fill=0.0, base=0, channel_multiplier=1).then_inc(ld, 1)
        p.inc(ld, 1)
    ld_pre = p.val(ld)

    @blk.sync
    def _(s):
        for c in range(8):
            s.dma_start(out=kf[:, c * TFULL:(c + 1) * TFULL],
                        in_=kF[c * 128:(c + 1) * 128, :]).then_inc(lf, 16)
            p.inc(lf, 16)
        for jb in range(32):
            s.dma_start(out=vT[:, jb * D:(jb + 1) * D],
                        in_=vTF[jb * 128:(jb + 1) * 128, :]).then_inc(lf, 16)
            p.inc(lf, 16)
        s.dma_start(out=bot[:], in_=boh[:]).then_inc(lf, 16)
        s.dma_start(out=gixt[:], in_=gixh[:]).then_inc(lf, 16)
        p.inc(lf, 32)
        s.dma_start(out=qc[:, 0:1024].rearrange("p (c t) -> p c t", c=8),
                    in_=qf.rearrange("(c p) t -> p c t", p=128)[:, :, 0:128]).then_inc(lf, 16)
        s.dma_start(out=xr[:, 0:1024].rearrange("p (c t) -> p c t", c=8),
                    in_=xo.rearrange("(c p) t -> p c t", p=128)[:, :, 0:128]).then_inc(lf, 16)
        p.inc(lf, 32)
    lf_pre = p.val(lf)

    dbg_so = [0]
    score_last_mm = [None] * NI
    outproj_last_ev = [None] * NI
    ctx_last_mm = [None] * NI
    trans_last_mm = [None] * NI
    exp_sx = [None] * NI
    norm_ev = [None] * NI
    lf_chunk = [lf_pre] * (NI + 1)

    for ic in range(NI):
        buf = ic % 2
        if ic + 1 < NI:
            @blk.sync
            def _(s, ic=ic):
                if ic >= 1:
                    s.wait_ge(mm, score_last_mm[ic - 1])
                    s.wait_ge(ev, outproj_last_ev[ic - 1])
                nb = (ic + 1) % 2
                s.dma_start(out=qc[:, nb * 1024:nb * 1024 + 1024].rearrange("p (c t) -> p c t", c=8),
                            in_=qf.rearrange("(c p) t -> p c t", p=128)[:, :, (ic + 1) * 128:(ic + 2) * 128]).then_inc(lf, 16)
                s.dma_start(out=xr[:, nb * 1024:nb * 1024 + 1024].rearrange("p (c t) -> p c t", c=8),
                            in_=xo.rearrange("(c p) t -> p c t", p=128)[:, :, (ic + 1) * 128:(ic + 2) * 128]).then_inc(lf, 16)
                p.inc(lf, 32)
        lf_chunk[ic + 1] = p.val(lf)

        def score_task(jc, ic=ic, buf=buf):
            def mmf(t, ps_):
                r = None
                for kc in range(8):
                    r = t.matmul(ps_[:, 0:512],
                                 qc[:, buf * 1024 + kc * 128: buf * 1024 + kc * 128 + 128],
                                 kf[:, kc * TFULL + jc * 512: kc * TFULL + jc * 512 + 512],
                                 start=(kc == 0), stop=(kc == 7))
                return r

            def evf(v, ps_, out):
                v.tensor_scalar(out=msk[:], in0=jio[:], scalar1=gixa[:, jc:jc + 1],
                                scalar2=NEG, op0=ALU.is_gt, op1=ALU.mult).then_inc(ev, 1)
                v.wait_ge(ev, p.inc(ev))
                return v.scalar_tensor_tensor(out=Ssb[:, jc * 512:(jc + 1) * 512],
                                              in0=ps_[:, 0:512], scalar=1.0, in1=msk[:],
                                              op0=ALU.mult, op1=ALU.add)
            return {"mm": mmf, "evac": evf}

        t_w = [(lf, lf_chunk[ic]), (so, dbg_so[0])]
        if ic == 0:
            t_w += [(ld, ld_pre)]
        v_w = []
        if ic > 0:
            v_w += [(sx, exp_sx[ic - 1])]

        @blk.vector
        def _(v, ic=ic, buf=buf):
            for s_, vv in v_w:
                v.wait_ge(s_, vv)
            if ic == 0:
                v.wait_ge(lf, lf_pre)
            v.tensor_tensor(out=gixa[:], in0=gixt[:, ic:ic + 1].broadcast_to([128, 8]),
                            in1=jbase[:], op=ALU.subtract).then_inc(ev, 1)
            v.wait_ge(ev, p.inc(ev))
        ev_gixa = p.val(ev)

        run_stream(p, blk, sems, [score_task(jc) for jc in range(8)], ring,
                   t_waits=t_w, v_waits=[(ev, ev_gixa)])
        score_last_mm[ic] = p.val(mm)


        @blk.vector
        def _(v, ic=ic):
            v.reduce_max(out=mx[:, 0:1], in_=Ssb[:], axis=AX.X).then_inc(ev, 1)
            v.wait_ge(ev, p.inc(ev))
            v.tensor_scalar(out=nmxb[:], in0=mx[:, 0:1], scalar1=-1.0, scalar2=None,
                            op0=ALU.mult).then_inc(ev, 1)
            p.inc(ev)
        ev_max = p.val(ev)

        @blk.scalar
        def _(s_, ic=ic):
            s_.wait_ge(ev, ev_max)
            s_.wait_ge(so, dbg_so[0])
            if ic > 0:
                s_.wait_ge(mm, trans_last_mm[ic - 1])
            s_.activation(out=wsb[:], in_=Ssb[:], func=AF.Exp, bias=nmxb[:],
                          scale=1.0).then_inc(sx, 1)
            p.inc(sx)
        exp_sx[ic] = p.val(sx)

        @blk.vector
        def _(v, ic=ic):
            v.wait_ge(sx, exp_sx[ic])
            v.reduce_sum(out=mx[:, 2:3], in_=wsb[:], axis=AX.X).then_inc(ev, 1)
            v.wait_ge(ev, p.inc(ev))
            v.reciprocal(out=mx[:, 3:4], in_=mx[:, 2:3]).then_inc(ev, 1)
            v.wait_ge(ev, p.inc(ev))
            v.tensor_scalar(out=wsb[:], in0=wsb[:], scalar1=mx[:, 3:4], scalar2=None,
                            op0=ALU.mult).then_inc(ev, 1)
            p.inc(ev)
        norm_ev[ic] = p.val(ev)


        def trans_task(jb, ic=ic):
            def mmf(t, ps_):
                return t.transpose(ps_[:, 0:128], wsb[:, jb * 128:(jb + 1) * 128], identb[:])

            def evf(v, ps_, out):
                return v.tensor_copy(out=wTt[:, jb * 128:(jb + 1) * 128], in_=ps_[:, 0:128])
            return {"mm": mmf, "evac": evf}

        tw = [(ev, norm_ev[ic])]
        if ic > 0:
            tw += [(mm, ctx_last_mm[ic - 1])]
        run_stream(p, blk, sems, [trans_task(jb) for jb in range(32)], pT, t_waits=tw)
        trans_last_mm[ic] = p.val(mm)


        def ctx_task(dvc, ic=ic):
            def mmf(t, ps_):
                r = None
                for jb in range(32):
                    r = t.matmul(ps_[:, 0:128],
                                 vT[:, jb * D + dvc * 128: jb * D + dvc * 128 + 128],
                                 wTt[:, jb * 128:(jb + 1) * 128],
                                 start=(jb == 0), stop=(jb == 31))
                return r

            def evf(v, ps_, out):
                return v.tensor_copy(out=ctxs[:, dvc * 128:(dvc + 1) * 128], in_=ps_[:, 0:128])
            return {"mm": mmf, "evac": evf}

        tw = []
        if ic > 0:
            pass
        run_stream(p, blk, sems, [ctx_task(dvc) for dvc in range(8)], ring, t_waits=tw)
        ctx_last_mm[ic] = p.val(mm)


        def op_task(doc, ic=ic, buf=buf):
            def mmf(t, ps_):
                r = None
                for dvc in range(8):
                    r = t.matmul(ps_[:, 0:128],
                                 wo[:, dvc * D + doc * 128: dvc * D + doc * 128 + 128],
                                 ctxs[:, dvc * 128:(dvc + 1) * 128],
                                 start=(dvc == 0), stop=(dvc == 7))
                return r

            def evf(v, ps_, out):
                v.tensor_scalar(out=out, in0=ps_[:, 0:128], scalar1=bot[:, doc:doc + 1],
                                scalar2=None, op0=ALU.add)
                return v.tensor_tensor(out=out, in0=out,
                                       in1=xr[:, buf * 1024 + doc * 128: buf * 1024 + doc * 128 + 128],
                                       op=ALU.add)
            return {"mm": mmf, "evac": evf,
                    "store": rO[doc * 128:(doc + 1) * 128, ic * 128:(ic + 1) * 128]}

        run_stream(p, blk, sems, [op_task(doc) for doc in range(8)], ring,
                   stg=stg, nslots=4, slot_w=128)
        outproj_last_ev[ic] = p.val(ev)

    @blk.sync
    def _(s):
        s.wait_ge(so, p.val(so))

    p.close()
    return nc



def _ns_B():
    pass

D = 1024
TOWN = 2048
HALO = 1088
XT = HALO + TOWN
QN = 2176
KN = 3072
HID2 = 2816
NC1 = 22

AB_SZ = 48640
AB_xinb = 0
AB_wsl = 25088
AB_wv = 27136
AB_csq = 31232
AB_ssq = 33408
AB_csk = 35584
AB_ssk = 38656
AB_o = 0
AB_o1 = 2048
AB_wsb = 2560
AB_wTt = 3584
AB_wo = 4608
AB_qa = 46592
AB_rot = 47104
AB_t1 = 47616
AB_t2 = 48128
AB_hn = 0
AB_hh = 4096
AB_w1 = 15360
AB_w2 = 19456
AB_Mloc = 41984
AB_pad = 46080

AF_SZ = 11272
AF_Ssb = 0
AF_xr = 1024
AF_sq = 5120
AF_stg = 6144
AF_P0r = 6144
AF_xp = 0
AF_x0 = 4104
AF_st3 = 8200
AF_ss2 = 0
AF_ssq = 6144
AF_stt0 = 0
AF_stt3 = 9224


def build_B(is_b2, dbg=False):
    p = Prog()
    nc = p.nc
    rin = nc.dram_tensor("rin", [D, XT], F32, kind="ExternalInput").ap()
    if is_b2:
        x0s_in = nc.dram_tensor("x0si", [D, TOWN + 1], F32, kind="ExternalInput").ap()
    wqkv = nc.dram_tensor("wqkv", [D, 1536], F32, kind="ExternalInput").ap()
    qsh = nc.dram_tensor("qsh", [128, 1], F32, kind="ExternalInput").ap()
    ksh = nc.dram_tensor("ksh", [128, 1], F32, kind="ExternalInput").ap()
    woD = nc.dram_tensor("woD", [512, D], F32, kind="ExternalInput").ap()
    lngh = nc.dram_tensor("lngh", [128, 8], F32, kind="ExternalInput").ap()
    lnbh = nc.dram_tensor("lnbh", [128, 8], F32, kind="ExternalInput").ap()
    w1D = nc.dram_tensor("w1D", [D, 2 * HID2], F32, kind="ExternalInput").ap()
    b1aD = nc.dram_tensor("b1aD", [128, NC1], F32, kind="ExternalInput").ap()
    b1gD = nc.dram_tensor("b1gD", [128, NC1], F32, kind="ExternalInput").ap()
    w2D = nc.dram_tensor("w2D", [HID2, D], F32, kind="ExternalInput").ap()
    b2D = nc.dram_tensor("b2D", [128, 8], F32, kind="ExternalInput").ap()
    CSqD = nc.dram_tensor("CSqD", [128, QN], F32, kind="ExternalInput").ap()
    SSqD = nc.dram_tensor("SSqD", [128, QN], F32, kind="ExternalInput").ap()
    CSkD = nc.dram_tensor("CSkD", [128, KN], F32, kind="ExternalInput").ap()
    SSkD = nc.dram_tensor("SSkD", [128, KN], F32, kind="ExternalInput").ap()
    MlocD = nc.dram_tensor("MlocD", [128, 4096], F32, kind="ExternalInput").ap()
    pad1D = nc.dram_tensor("pad1D", [128, 512], F32, kind="ExternalInput").ap()
    RTbD = nc.dram_tensor("RTbD", [128, 128], F32, kind="ExternalInput").ap()
    grpTD = nc.dram_tensor("grpTD", [2, 128], F32, kind="ExternalInput").ap()
    g64D = nc.dram_tensor("g64D", [128, 2], F32, kind="ExternalInput").ap()
    xout = nc.dram_tensor("xout", [D, TOWN], F32, kind="ExternalOutput").ap()
    if not is_b2:
        x0s_out = nc.dram_tensor("x0so", [D, TOWN + 1], F32, kind="ExternalOutput").ap()
    xpost_h = nc.dram_tensor("xpost_h", [D, TOWN + 1], F32, kind="Internal").ap()
    if dbg:
        dQ = nc.dram_tensor("dQ", [128, 4 * QN], BF16, kind="ExternalOutput").ap()
        dK = nc.dram_tensor("dK", [128, 4 * KN], BF16, kind="ExternalOutput").ap()
        dV = nc.dram_tensor("dV", [128, 24 * 512], BF16, kind="ExternalOutput").ap()
        dO = nc.dram_tensor("dO", [128, 4 * 512], BF16, kind="ExternalOutput").ap()
        dXP = nc.dram_tensor("dXP", [D, TOWN + 1], F32, kind="ExternalOutput").ap()
        dHN = nc.dram_tensor("dHN", [128, 8 * 512], BF16, kind="ExternalOutput").ap()
        dHH = nc.dram_tensor("dHH", [128, NC1 * 512], BF16, kind="ExternalOutput").ap()

    AB = p.sb("AB", [128, AB_SZ], BF16)
    AFa = p.sb("AFa", [128, AF_SZ], F32)
    qpp = p.sb("qpp", [128, 4 * QN], BF16)
    kpp = p.sb("kpp", [128, 4 * KN], BF16)
    vT = p.sb("vT", [128, 24 * 512], BF16)
    qst = p.sb("qst", [128, 1], F32)
    kst = p.sb("kst", [128, 1], F32)
    lngt = p.sb("lngt", [128, 8], F32)
    lnbt = p.sb("lnbt", [128, 8], F32)
    b1at = p.sb("b1at", [128, NC1], F32)
    b1gt = p.sb("b1gt", [128, NC1], F32)
    b2t = p.sb("b2t", [128, 8], F32)
    RTt = p.sb("RTt", [128, 128], BF16)
    grpTt = p.sb("grpTt", [2, 128], F32)
    g64t = p.sb("g64t", [128, 2], F32)
    ones1 = p.sb("onesB1", [1, 128], F32)
    onesc = p.sb("onesBc", [128, 1], F32)
    identb = p.sb("identbB", [128, 128], BF16)
    nmx = p.sb("nmxB", [128, 1], BF16)
    mxB = p.sb("mxBt", [128, 4], F32)

    ring = [p.ps(f"pm{i}", [128, 512], F32) for i in range(4)]
    pT = [p.ps(f"pT{i}", [128, 1024], BF16) for i in range(2)]
    pL1 = p.ps("pL1", [128, 512], F32)
    pL2 = p.ps("pL2", [128, 512], F32)

    blk = p.block()
    S = Sched(p, blk)

    RNG = lambda name, lo, hi: (name, lo, hi)
    rAB = lambda lo, hi: ("AB", lo, hi)
    rAF = lambda lo, hi: ("AF", lo, hi)
    ringR = [RNG(f"pm{i}", 0, 512) for i in range(4)]
    pTR = [RNG(f"pT{i}", 0, 128) for i in range(2)]

    def dmal(eng, out_ap, in_ap, wr, rd=()):
        S.emit(eng, lambda E: E.dma_start(out=out_ap, in_=in_ap), reads=rd, writes=[wr], inc=16)

    dmal("g", AB[:, AB_csq:AB_csq + QN], CSqD[:], rAB(AB_csq, AB_csq + QN))
    dmal("g", AB[:, AB_ssq:AB_ssq + QN], SSqD[:], rAB(AB_ssq, AB_ssq + QN))
    dmal("g", AB[:, AB_csk:AB_csk + KN], CSkD[:], rAB(AB_csk, AB_csk + KN))
    dmal("g", AB[:, AB_ssk:AB_ssk + KN], SSkD[:], rAB(AB_ssk, AB_ssk + KN))
    dmal("g", AB[:, AB_Mloc:AB_Mloc + 4096], MlocD[:], rAB(AB_Mloc, AB_Mloc + 4096))
    dmal("g", AB[:, AB_pad:AB_pad + 512], pad1D[:], rAB(AB_pad, AB_pad + 512))
    dmal("g", RTt[:], RTbD[:], RNG("RTt", 0, 128))
    for c in range(8):
        dmal("g", AB[:, AB_wv + c * 512:AB_wv + (c + 1) * 512],
             wqkv[c * 128:(c + 1) * 128, 1024:1536], rAB(AB_wv + c * 512, AB_wv + (c + 1) * 512))
    for t_, d_, nm in ((qst, qsh, "qst"), (kst, ksh, "kst"), (lngt, lngh, "lngt"),
                      (lnbt, lnbh, "lnbt"), (b1at, b1aD, "b1at"), (b1gt, b1gD, "b1gt"),
                      (b2t, b2D, "b2t"), (grpTt, grpTD, "grpTt"), (g64t, g64D, "g64t")):
        dmal("y", t_[:], d_[:], RNG(nm, 0, 64))
    S.emit("v", lambda v: v.memset(ones1[:], 1.0), writes=[RNG("ones1", 0, 128)])
    S.emit("v", lambda v: v.memset(onesc[:], 1.0), writes=[RNG("onesc", 0, 1)])

    def build_ident(g):
        g.memset(identb[:], 1.0)
        return g.affine_select(identb[:], identb[:], pattern=[[-1, 128]],
                               compare_op=ALU.is_equal, fill=0.0, base=0, channel_multiplier=1)
    S.emit("g", build_ident, writes=[RNG("identb", 0, 128)])

    if is_b2:
        for c in range(8):
            dmal("g", AB[:, AB_xinb + c * XT:AB_xinb + (c + 1) * XT],
                 rin[c * 128:(c + 1) * 128, :], rAB(AB_xinb + c * XT, AB_xinb + (c + 1) * XT))
    else:
        for tc in range(7):
            t0 = tc * 448
            for c in range(8):
                dmal("y", AFa[:, AF_P0r + c * 448:AF_P0r + (c + 1) * 448],
                     rin[c * 128:(c + 1) * 128, t0:t0 + 448],
                     rAF(AF_P0r + c * 448, AF_P0r + (c + 1) * 448))
            def s1f(t, t0=t0):
                r = None
                for c in range(8):
                    r = t.matmul(pL1[0:1, 0:448], onesc[:],
                                 AFa[:, AF_P0r + c * 448:AF_P0r + c * 448 + 448],
                                 start=(c == 0), stop=(c == 7))
                return r
            S.emit("t", s1f, reads=[rAF(AF_P0r, AF_P0r + 3584), RNG("onesc", 0, 1)],
                   writes=[RNG("pL1", 0, 448)])
            for c in range(8):
                sl = AF_sq + (c % 2) * 512
                S.emit("s", lambda s_, c=c, sl=sl: s_.activation(
                    out=AFa[:, sl:sl + 448], in_=AFa[:, AF_P0r + c * 448:AF_P0r + c * 448 + 448],
                    func=AF.Square), reads=[rAF(AF_P0r + c * 448, AF_P0r + c * 448 + 448)],
                    writes=[rAF(sl, sl + 448)])
                S.emit("t", lambda t, c=c, sl=sl: t.matmul(
                    pL2[0:1, 0:448], onesc[:], AFa[:, sl:sl + 448],
                    start=(c == 0), stop=(c == 7)), reads=[rAF(sl, sl + 448)],
                    writes=[RNG("pL2", 0, 448)])
            S.emit("v", lambda v: v.tensor_scalar(out=AFa[0:1, AF_stt0:AF_stt0 + 448], in0=pL1[0:1, 0:448],
                                                  scalar1=1.0 / 1024.0, scalar2=None, op0=ALU.mult),
                   reads=[RNG("pL1", 0, 448)], writes=[rAF(AF_stt0, AF_stt0 + 448)], small=True)
            S.emit("v", lambda v: v.tensor_tensor(out=AFa[0:1, AF_stt0 + 512:AF_stt0 + 960], in0=AFa[0:1, AF_stt0:AF_stt0 + 448],
                                                  in1=AFa[0:1, AF_stt0:AF_stt0 + 448], op=ALU.mult),
                   reads=[rAF(AF_stt0, AF_stt0 + 448)], writes=[rAF(AF_stt0 + 512, AF_stt0 + 960)], small=True)
            S.emit("v", lambda v: v.tensor_scalar(out=AFa[0:1, AF_stt0 + 1024:AF_stt0 + 1472], in0=pL2[0:1, 0:448],
                                                  scalar1=1.0 / 1024.0, scalar2=None, op0=ALU.mult),
                   reads=[RNG("pL2", 0, 448)], writes=[rAF(AF_stt0 + 1024, AF_stt0 + 1472)], small=True)
            S.emit("v", lambda v: v.tensor_tensor(out=AFa[0:1, AF_stt0 + 1024:AF_stt0 + 1472], in0=AFa[0:1, AF_stt0 + 1024:AF_stt0 + 1472],
                                                  in1=AFa[0:1, AF_stt0 + 512:AF_stt0 + 960], op=ALU.subtract),
                   reads=[rAF(AF_stt0 + 512, AF_stt0 + 1472)], writes=[rAF(AF_stt0 + 1024, AF_stt0 + 1472)], small=True)
            S.emit("v", lambda v: v.tensor_scalar(out=AFa[0:1, AF_stt0 + 1024:AF_stt0 + 1472],
                                                  in0=AFa[0:1, AF_stt0 + 1024:AF_stt0 + 1472],
                                                  scalar1=1e-5, scalar2=None, op0=ALU.add),
                   reads=[rAF(AF_stt0 + 1024, AF_stt0 + 1472)], writes=[rAF(AF_stt0 + 1024, AF_stt0 + 1472)], small=True)
            S.emit("s", lambda s_: s_.activation(out=AFa[0:1, AF_stt0 + 1536:AF_stt0 + 1984], in_=AFa[0:1, AF_stt0 + 1024:AF_stt0 + 1472],
                                                 func=AF.Sqrt),
                   reads=[rAF(AF_stt0 + 1024, AF_stt0 + 1472)], writes=[rAF(AF_stt0 + 1536, AF_stt0 + 1984)], small=True)
            S.emit("v", lambda v: v.reciprocal(out=AFa[0:1, AF_stt0 + 1536:AF_stt0 + 1984], in_=AFa[0:1, AF_stt0 + 1536:AF_stt0 + 1984]),
                   reads=[rAF(AF_stt0 + 1536, AF_stt0 + 1984)], writes=[rAF(AF_stt0 + 1536, AF_stt0 + 1984)], small=True)
            S.emit("t", lambda t: t.matmul(ring[0][:, 0:448], ones1[0:1, :], AFa[0:1, AF_stt0:AF_stt0 + 448],
                                           start=True, stop=True),
                   reads=[rAF(AF_stt0, AF_stt0 + 448)], writes=[ringR[0]])
            S.emit("t", lambda t: t.matmul(ring[1][:, 0:448], ones1[0:1, :], AFa[0:1, AF_stt0 + 1536:AF_stt0 + 1984],
                                           start=True, stop=True),
                   reads=[rAF(AF_stt0 + 1536, AF_stt0 + 1984)], writes=[ringR[1]])
            for c in range(8):
                sl = AF_sq + (c % 2) * 512
                S.emit("v", lambda v, c=c, sl=sl: v.tensor_tensor(
                    out=AFa[:, sl:sl + 448], in0=AFa[:, AF_P0r + c * 448:AF_P0r + c * 448 + 448],
                    in1=ring[0][:, 0:448], op=ALU.subtract),
                    reads=[rAF(AF_P0r + c * 448, AF_P0r + c * 448 + 448), ringR[0]],
                    writes=[rAF(sl, sl + 448)])
                S.emit("v", lambda v, c=c, t0=t0, sl=sl: v.tensor_tensor(
                    out=AB[:, AB_xinb + c * XT + t0:AB_xinb + c * XT + t0 + 448],
                    in0=AFa[:, sl:sl + 448], in1=ring[1][:, 0:448], op=ALU.mult),
                    reads=[rAF(sl, sl + 448), ringR[1]],
                    writes=[rAB(AB_xinb + c * XT + t0, AB_xinb + c * XT + t0 + 448)])
                lo = max(HALO - 1, t0)
                hi = t0 + 448
                if hi > lo:
                    o0 = AF_sq
                    S.emit("v", lambda v, c=c, lo=lo, hi=hi, t0=t0: v.tensor_tensor(
                        out=AFa[:, 4104:4104 + (hi - lo)],
                        in0=AFa[:, AF_P0r + c * 448 + (lo - t0):AF_P0r + c * 448 + (hi - t0)],
                        in1=ring[0][:, (lo - t0):(hi - t0)], op=ALU.subtract),
                        reads=[rAF(AF_P0r + c * 448 + (lo - t0), AF_P0r + c * 448 + (hi - t0)), ringR[0]],
                        writes=[rAF(4104, 4104 + (hi - lo))])
                    S.emit("v", lambda v, lo=lo, hi=hi, t0=t0: v.tensor_tensor(
                        out=AFa[:, 4104:4104 + (hi - lo)], in0=AFa[:, 4104:4104 + (hi - lo)],
                        in1=ring[1][:, (lo - t0):(hi - t0)], op=ALU.mult),
                        reads=[rAF(4104, 4104 + (hi - lo)), ringR[1]],
                        writes=[rAF(4104, 4104 + (hi - lo))])
                    S.emit("y", lambda s, c=c, lo=lo, hi=hi: s.dma_start(
                        out=x0s_out[c * 128:(c + 1) * 128, lo - (HALO - 1):hi - (HALO - 1)],
                        in_=AFa[:, 4104:4104 + (hi - lo)]),
                        reads=[rAF(4104, 4104 + (hi - lo))],
                        writes=[RNG("x0s", c * 4096 + lo, c * 4096 + hi)], inc=16)

    def xin_rhs(kc, x0, w):
        off = -1 if kc >= 4 else 0
        return AB[:, AB_xinb + kc * XT + x0 + off: AB_xinb + kc * XT + x0 + off + w]

    def xin_r(kc, x0, w):
        off = -1 if kc >= 4 else 0
        return rAB(AB_xinb + kc * XT + x0 + off, AB_xinb + kc * XT + x0 + off + w)

    wslot = [0]

    def load_wslice(colbase):
        slot = AB_wsl + (wslot[0] % 2) * 1024
        wslot[0] += 1
        for c in range(8):
            dmal("g", AB[:, slot + c * 128:slot + (c + 1) * 128],
                 wqkv[c * 128:(c + 1) * 128, colbase:colbase + 128],
                 rAB(slot + c * 128, slot + (c + 1) * 128))
        return slot

    def prep_side(side):
        if side == 0:
            xbase, width, out_t, outn, outw, cs_off, ss_off, sc, scR = (
                HALO - 128, QN, qpp, "qpp", QN, AB_csq, AB_ssq, qst, "qst")
        else:
            xbase, width, out_t, outn, outw, cs_off, ss_off, sc, scR = (
                HALO - 1024, KN, kpp, "kpp", KN, AB_csk, AB_ssk, kst, "kst")
        tcs = []
        t0 = 0
        while t0 < width:
            tw = min(512, width - t0)
            tcs.append((t0, tw))
            t0 += tw
        for fc in range(4):
            slot = load_wslice(side * 512 + fc * 128)
            for (t0, tw) in tcs:
                def rawf(t, slot=slot, t0=t0, tw=tw):
                    r = None
                    for kc in range(8):
                        r = t.matmul(ring[0][:, 0:tw],
                                     AB[:, slot + kc * 128:slot + (kc + 1) * 128],
                                     xin_rhs(kc, xbase + t0, tw),
                                     start=(kc == 0), stop=(kc == 7))
                    return r
                S.emit("t", rawf,
                       reads=[rAB(slot, slot + 1024)] + [xin_r(kc, xbase + t0, tw) for kc in range(8)],
                       writes=[ringR[0]])
                S.emit("v", lambda v, tw=tw: v.tensor_copy(out=AB[:, AB_qa:AB_qa + tw],
                                                           in_=ring[0][:, 0:tw]),
                       reads=[ringR[0]], writes=[rAB(AB_qa, AB_qa + tw)])
                S.emit("s", lambda s_, tw=tw: s_.activation(out=AFa[:, AF_sq:AF_sq + tw],
                                                            in_=AB[:, AB_qa:AB_qa + tw], func=AF.Square),
                       reads=[rAB(AB_qa, AB_qa + tw)], writes=[rAF(AF_sq, AF_sq + tw)])
                S.emit("t", lambda t, tw=tw: t.matmul(ring[1][0:2, 0:tw], g64t[:, 0:2],
                                                      AFa[:, AF_sq:AF_sq + tw], start=True, stop=True),
                       reads=[rAF(AF_sq, AF_sq + tw), RNG("g64t", 0, 64)], writes=[ringR[1]])
                S.emit("v", lambda v, t0=t0, tw=tw: v.tensor_scalar_max(
                    out=AFa[0:2, AF_ssq + t0:AF_ssq + t0 + tw], in0=ring[1][0:2, 0:tw], scalar1=1e-24),
                    reads=[ringR[1]], writes=[rAF(AF_ssq + t0, AF_ssq + t0 + tw)], small=True)
                S.emit("s", lambda s_, t0=t0, tw=tw: s_.activation(
                    out=AFa[0:2, AF_ss2 + t0:AF_ss2 + t0 + tw], in_=AFa[0:2, AF_ssq + t0:AF_ssq + t0 + tw], func=AF.Sqrt),
                    reads=[rAF(AF_ssq + t0, AF_ssq + t0 + tw)], writes=[rAF(AF_ss2 + t0, AF_ss2 + t0 + tw)], small=True)
                S.emit("v", lambda v, t0=t0, tw=tw: v.reciprocal(
                    out=AFa[0:2, AF_ss2 + t0:AF_ss2 + t0 + tw], in_=AFa[0:2, AF_ss2 + t0:AF_ss2 + t0 + tw]),
                    reads=[rAF(AF_ss2 + t0, AF_ss2 + t0 + tw)], writes=[rAF(AF_ss2 + t0, AF_ss2 + t0 + tw)], small=True)
                S.emit("t", lambda t, t0=t0, tw=tw: t.matmul(
                    ring[2][:, 0:tw], grpTt[0:2, :], AFa[0:2, AF_ss2 + t0:AF_ss2 + t0 + tw], start=True, stop=True),
                    reads=[rAF(AF_ss2 + t0, AF_ss2 + t0 + tw), RNG("grpTt", 0, 128)], writes=[ringR[2]])
                S.emit("v", lambda v, tw=tw, sc=sc: v.scalar_tensor_tensor(
                    out=AB[:, AB_qa:AB_qa + tw], in0=AB[:, AB_qa:AB_qa + tw], scalar=sc[:, 0:1],
                    in1=ring[2][:, 0:tw], op0=ALU.mult, op1=ALU.mult),
                    reads=[rAB(AB_qa, AB_qa + tw), ringR[2], RNG(scR, 0, 1)],
                    writes=[rAB(AB_qa, AB_qa + tw)])
                S.emit("t", lambda t, tw=tw: t.matmul(ring[3][:, 0:tw], RTt[:],
                                                      AB[:, AB_qa:AB_qa + tw], start=True, stop=True),
                       reads=[rAB(AB_qa, AB_qa + tw), RNG("RTt", 0, 128)], writes=[ringR[3]])
                S.emit("v", lambda v, tw=tw: v.tensor_copy(out=AB[:, AB_rot:AB_rot + tw],
                                                           in_=ring[3][:, 0:tw]),
                       reads=[ringR[3]], writes=[rAB(AB_rot, AB_rot + tw)])
                S.emit("v", lambda v, t0=t0, tw=tw: v.tensor_tensor(
                    out=AB[:, AB_t1:AB_t1 + tw], in0=AB[:, AB_qa:AB_qa + tw],
                    in1=AB[:, cs_off + t0:cs_off + t0 + tw], op=ALU.mult),
                    reads=[rAB(AB_qa, AB_qa + tw), rAB(cs_off + t0, cs_off + t0 + tw)],
                    writes=[rAB(AB_t1, AB_t1 + tw)])
                S.emit("v", lambda v, t0=t0, tw=tw: v.tensor_tensor(
                    out=AB[:, AB_t2:AB_t2 + tw], in0=AB[:, AB_rot:AB_rot + tw],
                    in1=AB[:, ss_off + t0:ss_off + t0 + tw], op=ALU.mult),
                    reads=[rAB(AB_rot, AB_rot + tw), rAB(ss_off + t0, ss_off + t0 + tw)],
                    writes=[rAB(AB_t2, AB_t2 + tw)])
                S.emit("v", lambda v, fc=fc, t0=t0, tw=tw: v.tensor_tensor(
                    out=out_t[:, fc * outw + t0:fc * outw + t0 + tw],
                    in0=AB[:, AB_t1:AB_t1 + tw], in1=AB[:, AB_t2:AB_t2 + tw], op=ALU.add),
                    reads=[rAB(AB_t1, AB_t1 + tw), rAB(AB_t2, AB_t2 + tw)],
                    writes=[RNG(outn, fc * outw + t0, fc * outw + t0 + tw)])

    prep_side(0)
    prep_side(1)

    for jb in range(24):
        def vtf(t, jb=jb):
            r = None
            for kc in range(8):
                r = t.matmul(ring[0][:, 0:512], xin_rhs(kc, (HALO - 1024) + jb * 128, 128),
                             AB[:, AB_wv + kc * 512:AB_wv + (kc + 1) * 512],
                             start=(kc == 0), stop=(kc == 7))
            return r
        S.emit("t", vtf, reads=[xin_r(kc, (HALO - 1024) + jb * 128, 128) for kc in range(8)] +
               [rAB(AB_wv, AB_wv + 4096)], writes=[ringR[0]])
        S.emit("v", lambda v, jb=jb: v.tensor_copy(out=vT[:, jb * 512:(jb + 1) * 512],
                                                   in_=ring[0][:, 0:512]),
               reads=[ringR[0]], writes=[RNG("vT", jb * 512, (jb + 1) * 512)])

    if dbg:
        S.emit("y", lambda s: s.dma_start(out=dQ[:], in_=qpp[:]),
               reads=[RNG("qpp", 0, 4 * QN)], writes=[RNG("zdQ", 0, 1)], inc=16)
        S.emit("y", lambda s: s.dma_start(out=dK[:], in_=kpp[:]),
               reads=[RNG("kpp", 0, 4 * KN)], writes=[RNG("zdK", 0, 1)], inc=16)
        S.emit("y", lambda s: s.dma_start(out=dV[:], in_=vT[:]),
               reads=[RNG("vT", 0, 24 * 512)], writes=[RNG("zdV", 0, 1)], inc=16)
    S.barrier()

    for c in range(4):
        dmal("g", AB[:, AB_wo + c * D:AB_wo + (c + 1) * D],
             woD[c * 128:(c + 1) * 128, :], rAB(AB_wo + c * D, AB_wo + (c + 1) * D))

    chunks = [(-1, 3)] + [(w, icl) for w in range(4) for icl in range(4)]

    def attn_chunk(w, icl):
        qcol = 128 + w * 512 + icl * 128
        kcol = 512 * (w + 1)
        for h in range(8):
            qc, row = h // 2, (h % 2) * 64
            qre = RNG("qpp", qc * QN + qcol, qc * QN + qcol + 128)
            kre = RNG("kpp", qc * KN + kcol, qc * KN + kcol + 1024)

            def scf(t, qc=qc, row=row, qcol=qcol, kcol=kcol):
                t.matmul(ring[0][:, 0:512], qpp[row:row + 64, qc * QN + qcol:qc * QN + qcol + 128],
                         kpp[row:row + 64, qc * KN + kcol:qc * KN + kcol + 512], start=True, stop=True)
                return t.matmul(ring[1][:, 0:512], qpp[row:row + 64, qc * QN + qcol:qc * QN + qcol + 128],
                                kpp[row:row + 64, qc * KN + kcol + 512:qc * KN + kcol + 1024],
                                start=True, stop=True)
            S.emit("t", scf, reads=[qre, kre], writes=[ringR[0], ringR[1]])

            for half in range(2):
                def evf(v, half=half, icl=icl):
                    return v.scalar_tensor_tensor(
                        out=AFa[:, AF_Ssb + half * 512:AF_Ssb + half * 512 + 512],
                        in0=ring[half][:, 0:512], scalar=1.0,
                        in1=AB[:, AB_Mloc + icl * 1024 + half * 512:AB_Mloc + icl * 1024 + half * 512 + 512],
                        op0=ALU.mult, op1=ALU.add)
                S.emit("v", evf, reads=[ringR[half], rAB(AB_Mloc + icl * 1024 + half * 512,
                                                         AB_Mloc + icl * 1024 + half * 512 + 512)],
                       writes=[rAF(AF_Ssb + half * 512, AF_Ssb + half * 512 + 512)])
            if w == 0:
                S.emit("v", lambda v: v.tensor_tensor(out=AFa[:, AF_Ssb:AF_Ssb + 512],
                                                      in0=AFa[:, AF_Ssb:AF_Ssb + 512],
                                                      in1=AB[:, AB_pad:AB_pad + 512], op=ALU.add),
                       reads=[rAF(AF_Ssb, AF_Ssb + 512), rAB(AB_pad, AB_pad + 512)],
                       writes=[rAF(AF_Ssb, AF_Ssb + 512)])
            S.emit("v", lambda v: v.reduce_max(out=mxB[:, 0:1], in_=AFa[:, AF_Ssb:AF_Ssb + 1024],
                                               axis=AX.X),
                   reads=[rAF(AF_Ssb, AF_Ssb + 1024)], writes=[RNG("mxB", 0, 1)], small=True)
            S.emit("v", lambda v: v.tensor_scalar(out=mxB[:, 1:2], in0=mxB[:, 0:1], scalar1=-1.0,
                                                  scalar2=None, op0=ALU.mult),
                   reads=[RNG("mxB", 0, 1)], writes=[RNG("mxB", 1, 2)], small=True)
            S.emit("s", lambda s_: s_.activation(out=AB[:, AB_wsb:AB_wsb + 1024],
                                                 in_=AFa[:, AF_Ssb:AF_Ssb + 1024], func=AF.Exp,
                                                 bias=mxB[:, 1:2], scale=1.0),
                   reads=[rAF(AF_Ssb, AF_Ssb + 1024), RNG("mxB", 1, 2)],
                   writes=[rAB(AB_wsb, AB_wsb + 1024)])
            S.emit("v", lambda v: v.reduce_sum(out=mxB[:, 2:3], in_=AB[:, AB_wsb:AB_wsb + 1024],
                                               axis=AX.X),
                   reads=[rAB(AB_wsb, AB_wsb + 1024)], writes=[RNG("mxB", 2, 3)], small=True)
            S.emit("v", lambda v: v.reciprocal(out=mxB[:, 3:4], in_=mxB[:, 2:3]),
                   reads=[RNG("mxB", 2, 3)], writes=[RNG("mxB", 3, 4)], small=True)
            S.emit("v", lambda v: v.tensor_scalar(out=AB[:, AB_wsb:AB_wsb + 1024],
                                                  in0=AB[:, AB_wsb:AB_wsb + 1024],
                                                  scalar1=mxB[:, 3:4], scalar2=None, op0=ALU.mult),
                   reads=[rAB(AB_wsb, AB_wsb + 1024), RNG("mxB", 3, 4)],
                   writes=[rAB(AB_wsb, AB_wsb + 1024)])
            for jb in range(8):
                S.emit("t", lambda t, jb=jb: t.transpose(pT[jb % 2][:, 0:128],
                                                         AB[:, AB_wsb + jb * 128:AB_wsb + (jb + 1) * 128],
                                                         identb[:]),
                       reads=[rAB(AB_wsb + jb * 128, AB_wsb + (jb + 1) * 128), RNG("identb", 0, 128)],
                       writes=[pTR[jb % 2]])
                S.emit("v", lambda v, jb=jb: v.tensor_copy(out=AB[:, AB_wTt + jb * 128:AB_wTt + (jb + 1) * 128],
                                                           in_=pT[jb % 2][:, 0:128]),
                       reads=[pTR[jb % 2]], writes=[rAB(AB_wTt + jb * 128, AB_wTt + (jb + 1) * 128)])

            def ctxf(t, kcol=kcol, h=h):
                r = None
                for jb in range(8):
                    rb = kcol // 128 + jb
                    r = t.matmul(ring[2][0:64, 0:128],
                                 vT[:, rb * 512 + h * 64: rb * 512 + h * 64 + 64],
                                 AB[:, AB_wTt + jb * 128:AB_wTt + (jb + 1) * 128],
                                 start=(jb == 0), stop=(jb == 7))
                return r
            S.emit("t", ctxf, reads=[rAB(AB_wTt, AB_wTt + 1024),
                                     RNG("vT", (kcol // 128) * 512, (kcol // 128 + 8) * 512)],
                   writes=[RNG("pm2", 0, 128)])
            ocol = (AB_o1 + qc * 128) if w < 0 else (AB_o + qc * 512 + icl * 128)
            S.emit("v", lambda v, row=row, ocol=ocol: v.tensor_copy(
                out=AB[row:row + 64, ocol:ocol + 128], in_=ring[2][0:64, 0:128]),
                reads=[RNG("pm2", 0, 128)], writes=[rAB(ocol, ocol + 128)])

    def outproj(w):
        if w < 0:
            osrc, tw, xp_lo = AB_o1, 128, 0
        else:
            osrc, tw, xp_lo = AB_o, 512, 1 + w * 512
        for doc in range(8):
            def opf(t, doc=doc, osrc=osrc, tw=tw):
                r = None
                for kc in range(4):
                    r = t.matmul(ring[(doc % 2)][:, 0:tw],
                                 AB[:, AB_wo + kc * D + doc * 128:AB_wo + kc * D + doc * 128 + 128],
                                 AB[:, osrc + kc * tw:osrc + (kc + 1) * tw],
                                 start=(kc == 0), stop=(kc == 3))
                return r
            S.emit("t", opf, reads=[rAB(AB_wo, AB_wo + 4096), rAB(osrc, osrc + 4 * tw)],
                   writes=[ringR[doc % 2]])
            if w < 0:
                S.emit("v", lambda v, doc=doc: v.tensor_tensor(
                    out=AFa[:, AF_stg + (doc % 4) * 512:AF_stg + (doc % 4) * 512 + 1],
                    in0=ring[doc % 2][:, 127:128], in1=AFa[:, AF_xr + doc:AF_xr + doc + 1],
                    op=ALU.add),
                    reads=[ringR[doc % 2], rAF(AF_xr + doc, AF_xr + doc + 1)],
                    writes=[rAF(AF_stg + (doc % 4) * 512, AF_stg + (doc % 4) * 512 + 1)])
                S.emit("y", lambda s, doc=doc: s.dma_start(
                    out=xpost_h[doc * 128:(doc + 1) * 128, 0:1],
                    in_=AFa[:, AF_stg + (doc % 4) * 512:AF_stg + (doc % 4) * 512 + 1],
                    allow_slow_non_contiguous=True),
                    reads=[rAF(AF_stg + (doc % 4) * 512, AF_stg + (doc % 4) * 512 + 1)],
                    writes=[RNG("xpost", doc * 4096 + 0, doc * 4096 + 1)], inc=16)
            else:
                S.emit("v", lambda v, doc=doc, tw=tw: v.tensor_tensor(
                    out=AFa[:, AF_stg + (doc % 4) * 512:AF_stg + (doc % 4) * 512 + tw],
                    in0=ring[doc % 2][:, 0:tw],
                    in1=AFa[:, AF_xr + doc * 512:AF_xr + doc * 512 + tw], op=ALU.add),
                    reads=[ringR[doc % 2], rAF(AF_xr + doc * 512, AF_xr + doc * 512 + tw)],
                    writes=[rAF(AF_stg + (doc % 4) * 512, AF_stg + (doc % 4) * 512 + tw)])
                S.emit("y", lambda s, doc=doc, xp_lo=xp_lo, tw=tw: s.dma_start(
                    out=xpost_h[doc * 128:(doc + 1) * 128, xp_lo:xp_lo + tw],
                    in_=AFa[:, AF_stg + (doc % 4) * 512:AF_stg + (doc % 4) * 512 + tw]),
                    reads=[rAF(AF_stg + (doc % 4) * 512, AF_stg + (doc % 4) * 512 + tw)],
                    writes=[RNG("xpost", doc * 4096 + xp_lo, doc * 4096 + xp_lo + tw)], inc=16)

    for (w, icl) in chunks:
        attn_chunk(w, icl)
        if dbg and w == 0 and icl == 3:
            S.emit("y", lambda s: s.dma_start(out=dO[:], in_=AB[:, AB_o:AB_o + 2048]),
                   reads=[rAB(AB_o, AB_o + 2048)], writes=[RNG("zdO", 0, 1)], inc=16)
        last_of_window = (w < 0) or (icl == 3)
        if last_of_window:
            if w < 0:
                for c in range(8):
                    src = (rin[c * 128:(c + 1) * 128, HALO - 1:HALO]
                           if is_b2 else x0s_out[c * 128:(c + 1) * 128, 0:1])
                    S.emit("y", lambda s, c=c, src=src: s.dma_start(
                        out=AFa[:, AF_xr + c:AF_xr + c + 1], in_=src,
                        allow_slow_non_contiguous=True),
                        reads=[RNG("x0s", c * 4096 + HALO - 1, c * 4096 + HALO)] if not is_b2 else [],
                        writes=[rAF(AF_xr + c, AF_xr + c + 1)], inc=16)
            else:
                for c in range(8):
                    src = (rin[c * 128:(c + 1) * 128, HALO + w * 512:HALO + (w + 1) * 512]
                           if is_b2 else x0s_out[c * 128:(c + 1) * 128, 1 + w * 512:1 + (w + 1) * 512])
                    S.emit("y", lambda s, c=c, src=src: s.dma_start(
                        out=AFa[:, AF_xr + c * 512:AF_xr + (c + 1) * 512], in_=src),
                        reads=[RNG("x0s", c * 4096 + HALO + w * 512, c * 4096 + HALO + (w + 1) * 512)] if not is_b2 else [],
                        writes=[rAF(AF_xr + c * 512, AF_xr + (c + 1) * 512)], inc=16)
            outproj(w)

    S.barrier()

    if dbg:
        for c in range(8):
            S.emit("y", lambda s, c=c: s.dma_start(out=dXP[c * 128:(c + 1) * 128, :],
                                                   in_=xpost_h[c * 128:(c + 1) * 128, :]),
                   reads=[RNG("xpost", c * 4096, c * 4096 + 2049)],
                   writes=[RNG("zdXP", 0, 1)], inc=16)

    for c in range(NC1):
        dmal("g", AB[:, AB_w2 + c * 1024:AB_w2 + (c + 1) * 1024],
             w2D[c * 128:(c + 1) * 128, :], rAB(AB_w2 + c * 1024, AB_w2 + (c + 1) * 1024))

    for tc in range(4):
        xp0 = tc * 512
        for c in range(8):
            S.emit("y", lambda s, c=c, xp0=xp0: s.dma_start(
                out=AFa[:, AF_xp + c * 513:AF_xp + (c + 1) * 513],
                in_=xpost_h[c * 128:(c + 1) * 128, xp0:xp0 + 513]),
                reads=[RNG("xpost", c * 4096 + xp0, c * 4096 + xp0 + 513)],
                writes=[rAF(AF_xp + c * 513, AF_xp + (c + 1) * 513)], inc=16)
        if is_b2:
            for c in range(8):
                S.emit("y", lambda s, c=c, xp0=xp0: s.dma_start(
                    out=AFa[:, AF_x0 + c * 512:AF_x0 + (c + 1) * 512],
                    in_=x0s_in[c * 128:(c + 1) * 128, 1 + xp0:1 + xp0 + 512]),
                    writes=[rAF(AF_x0 + c * 512, AF_x0 + (c + 1) * 512)], inc=16)

        def h2slice(kc):
            off = 1 if kc < 4 else 0
            return AFa[:, AF_xp + kc * 513 + off:AF_xp + kc * 513 + off + 512]

        def h2r(kc):
            off = 1 if kc < 4 else 0
            return rAF(AF_xp + kc * 513 + off, AF_xp + kc * 513 + off + 512)

        def s1f(t):
            r = None
            for kc in range(8):
                r = t.matmul(pL1[0:1, 0:512], onesc[:], h2slice(kc),
                             start=(kc == 0), stop=(kc == 7))
            return r
        S.emit("t", s1f, reads=[h2r(kc) for kc in range(8)] + [RNG("onesc", 0, 1)],
               writes=[RNG("pL1", 0, 512)])
        for kc in range(8):
            sl = AF_st3 + (kc % 2) * 512
            S.emit("s", lambda s_, kc=kc, sl=sl: s_.activation(out=AFa[:, sl:sl + 512],
                                                               in_=h2slice(kc), func=AF.Square),
                   reads=[h2r(kc)], writes=[rAF(sl, sl + 512)])
            S.emit("t", lambda t, kc=kc, sl=sl: t.matmul(pL2[0:1, 0:512], onesc[:],
                                                         AFa[:, sl:sl + 512],
                                                         start=(kc == 0), stop=(kc == 7)),
                   reads=[rAF(sl, sl + 512)], writes=[RNG("pL2", 0, 512)])
        S.emit("v", lambda v: v.tensor_scalar(out=AFa[0:1, AF_stt3:AF_stt3 + 512], in0=pL1[0:1, 0:512],
                                              scalar1=1.0 / 1024.0, scalar2=None, op0=ALU.mult),
               reads=[RNG("pL1", 0, 512)], writes=[rAF(AF_stt3, AF_stt3 + 512)], small=True)
        S.emit("v", lambda v: v.tensor_tensor(out=AFa[0:1, AF_stt3 + 512:AF_stt3 + 1024], in0=AFa[0:1, AF_stt3:AF_stt3 + 512],
                                              in1=AFa[0:1, AF_stt3:AF_stt3 + 512], op=ALU.mult),
               reads=[rAF(AF_stt3, AF_stt3 + 512)], writes=[rAF(AF_stt3 + 512, AF_stt3 + 1024)], small=True)
        S.emit("v", lambda v: v.tensor_scalar(out=AFa[0:1, AF_stt3 + 1024:AF_stt3 + 1536], in0=pL2[0:1, 0:512],
                                              scalar1=1.0 / 1024.0, scalar2=None, op0=ALU.mult),
               reads=[RNG("pL2", 0, 512)], writes=[rAF(AF_stt3 + 1024, AF_stt3 + 1536)], small=True)
        S.emit("v", lambda v: v.tensor_tensor(out=AFa[0:1, AF_stt3 + 1024:AF_stt3 + 1536], in0=AFa[0:1, AF_stt3 + 1024:AF_stt3 + 1536],
                                              in1=AFa[0:1, AF_stt3 + 512:AF_stt3 + 1024], op=ALU.subtract),
               reads=[rAF(AF_stt3 + 512, AF_stt3 + 1536)], writes=[rAF(AF_stt3 + 1024, AF_stt3 + 1536)], small=True)
        S.emit("v", lambda v: v.tensor_scalar(out=AFa[0:1, AF_stt3 + 1024:AF_stt3 + 1536],
                                              in0=AFa[0:1, AF_stt3 + 1024:AF_stt3 + 1536],
                                              scalar1=1e-5, scalar2=None, op0=ALU.add),
               reads=[rAF(AF_stt3 + 1024, AF_stt3 + 1536)], writes=[rAF(AF_stt3 + 1024, AF_stt3 + 1536)], small=True)
        S.emit("s", lambda s_: s_.activation(out=AFa[0:1, AF_stt3 + 1536:AF_stt3 + 2048], in_=AFa[0:1, AF_stt3 + 1024:AF_stt3 + 1536],
                                             func=AF.Sqrt),
               reads=[rAF(AF_stt3 + 1024, AF_stt3 + 1536)], writes=[rAF(AF_stt3 + 1536, AF_stt3 + 2048)], small=True)
        S.emit("v", lambda v: v.reciprocal(out=AFa[0:1, AF_stt3 + 1536:AF_stt3 + 2048], in_=AFa[0:1, AF_stt3 + 1536:AF_stt3 + 2048]),
               reads=[rAF(AF_stt3 + 1536, AF_stt3 + 2048)], writes=[rAF(AF_stt3 + 1536, AF_stt3 + 2048)], small=True)
        S.emit("t", lambda t: t.matmul(ring[0][:, 0:512], ones1[0:1, :], AFa[0:1, AF_stt3:AF_stt3 + 512],
                                       start=True, stop=True),
               reads=[rAF(AF_stt3, AF_stt3 + 512)], writes=[ringR[0]])
        S.emit("t", lambda t: t.matmul(ring[1][:, 0:512], ones1[0:1, :], AFa[0:1, AF_stt3 + 1536:AF_stt3 + 2048],
                                       start=True, stop=True),
               reads=[rAF(AF_stt3 + 1536, AF_stt3 + 2048)], writes=[ringR[1]])
        for kc in range(8):
            sl = AF_st3 + (kc % 2) * 512
            S.emit("v", lambda v, kc=kc, sl=sl: v.tensor_tensor(
                out=AFa[:, sl:sl + 512], in0=h2slice(kc), in1=ring[0][:, 0:512], op=ALU.subtract),
                reads=[h2r(kc), ringR[0]], writes=[rAF(sl, sl + 512)])
            S.emit("v", lambda v, kc=kc, sl=sl: v.tensor_tensor(
                out=AFa[:, sl:sl + 512], in0=AFa[:, sl:sl + 512], in1=ring[1][:, 0:512], op=ALU.mult),
                reads=[rAF(sl, sl + 512), ringR[1]], writes=[rAF(sl, sl + 512)])
            S.emit("v", lambda v, kc=kc, sl=sl: v.tensor_scalar(
                out=AB[:, AB_hn + kc * 512:AB_hn + (kc + 1) * 512], in0=AFa[:, sl:sl + 512],
                scalar1=lngt[:, kc:kc + 1], scalar2=lnbt[:, kc:kc + 1], op0=ALU.mult, op1=ALU.add),
                reads=[rAF(sl, sl + 512), RNG("lngt", 0, 8), RNG("lnbt", 0, 8)],
                writes=[rAB(AB_hn + kc * 512, AB_hn + (kc + 1) * 512)])

        if dbg and tc == 0:
            S.emit("y", lambda s: s.dma_start(out=dHN[:], in_=AB[:, AB_hn:AB_hn + 4096]),
                   reads=[rAB(AB_hn, AB_hn + 4096)], writes=[RNG("zdHN", 0, 1)], inc=16)
        for c in range(NC1):
            slot = AB_w1 + (c % 2) * 2048
            for half, colbase in ((0, c * 128), (1, HID2 + c * 128)):
                for kc in range(8):
                    dmal("g", AB[:, slot + half * 1024 + kc * 128:slot + half * 1024 + (kc + 1) * 128],
                         w1D[kc * 128:(kc + 1) * 128, colbase:colbase + 128],
                         rAB(slot + half * 1024 + kc * 128, slot + half * 1024 + (kc + 1) * 128))

            def af(t, slot=slot):
                r = None
                for kc in range(8):
                    r = t.matmul(ring[2][:, 0:512],
                                 AB[:, slot + kc * 128:slot + (kc + 1) * 128],
                                 AB[:, AB_hn + kc * 512:AB_hn + (kc + 1) * 512],
                                 start=(kc == 0), stop=(kc == 7))
                return r
            S.emit("t", af, reads=[rAB(slot, slot + 1024), rAB(AB_hn, AB_hn + 4096)],
                   writes=[ringR[2]])

            def gf(t, slot=slot):
                r = None
                for kc in range(8):
                    r = t.matmul(ring[3][:, 0:512],
                                 AB[:, slot + 1024 + kc * 128:slot + 1024 + (kc + 1) * 128],
                                 AB[:, AB_hn + kc * 512:AB_hn + (kc + 1) * 512],
                                 start=(kc == 0), stop=(kc == 7))
                return r
            S.emit("t", gf, reads=[rAB(slot + 1024, slot + 2048), rAB(AB_hn, AB_hn + 4096)],
                   writes=[ringR[3]])
            S.emit("v", lambda v, c=c: v.tensor_scalar(out=AB[:, AB_qa:AB_qa + 512],
                                                       in0=ring[2][:, 0:512],
                                                       scalar1=b1at[:, c:c + 1], scalar2=None,
                                                       op0=ALU.add),
                   reads=[ringR[2], RNG("b1at", 0, NC1)], writes=[rAB(AB_qa, AB_qa + 512)])
            S.emit("s", lambda s_, c=c: s_.activation(out=AB[:, AB_rot:AB_rot + 512],
                                                      in_=ring[3][:, 0:512], func=AF.Gelu,
                                                      bias=b1gt[:, c:c + 1], scale=1.0),
                   reads=[ringR[3], RNG("b1gt", 0, NC1)], writes=[rAB(AB_rot, AB_rot + 512)])
            S.emit("v", lambda v, c=c: v.tensor_tensor(out=AB[:, AB_hh + c * 512:AB_hh + (c + 1) * 512],
                                                       in0=AB[:, AB_qa:AB_qa + 512],
                                                       in1=AB[:, AB_rot:AB_rot + 512], op=ALU.mult),
                   reads=[rAB(AB_qa, AB_qa + 512), rAB(AB_rot, AB_rot + 512)],
                   writes=[rAB(AB_hh + c * 512, AB_hh + (c + 1) * 512)])

        if dbg and tc == 0:
            S.emit("y", lambda s: s.dma_start(out=dHH[:], in_=AB[:, AB_hh:AB_hh + NC1 * 512]),
                   reads=[rAB(AB_hh, AB_hh + NC1 * 512)], writes=[RNG("zdHH", 0, 1)], inc=16)
        for doc in range(8):
            def w2f(t, doc=doc):
                r = None
                for c in range(NC1):
                    r = t.matmul(ring[doc % 2][:, 0:512],
                                 AB[:, AB_w2 + c * 1024 + doc * 128:AB_w2 + c * 1024 + doc * 128 + 128],
                                 AB[:, AB_hh + c * 512:AB_hh + (c + 1) * 512],
                                 start=(c == 0), stop=(c == NC1 - 1))
                return r
            S.emit("t", w2f, reads=[rAB(AB_w2, AB_w2 + NC1 * 1024), rAB(AB_hh, AB_hh + NC1 * 512)],
                   writes=[ringR[doc % 2]])
            stg = AF_st3 + (doc % 2) * 512

            def evf(v, doc=doc, stg=stg):
                return v.scalar_tensor_tensor(
                    out=AFa[:, stg:stg + 512], in0=ring[doc % 2][:, 0:512],
                    scalar=b2t[:, doc:doc + 1],
                    in1=AFa[:, AF_xp + doc * 513 + 1:AF_xp + doc * 513 + 513],
                    op0=ALU.add, op1=ALU.add)
            S.emit("v", evf, reads=[ringR[doc % 2], RNG("b2t", 0, 8),
                                    rAF(AF_xp + doc * 513 + 1, AF_xp + doc * 513 + 513)],
                   writes=[rAF(stg, stg + 512)])
            if is_b2:
                S.emit("v", lambda v, doc=doc, stg=stg: v.tensor_tensor(
                    out=AFa[:, stg:stg + 512], in0=AFa[:, stg:stg + 512],
                    in1=AFa[:, AF_x0 + doc * 512:AF_x0 + (doc + 1) * 512], op=ALU.add),
                    reads=[rAF(stg, stg + 512), rAF(AF_x0 + doc * 512, AF_x0 + (doc + 1) * 512)],
                    writes=[rAF(stg, stg + 512)])
            S.emit("y", lambda s, doc=doc, tc=tc, stg=stg: s.dma_start(
                out=xout[doc * 128:(doc + 1) * 128, tc * 512:(tc + 1) * 512],
                in_=AFa[:, stg:stg + 512]),
                reads=[rAF(stg, stg + 512)], writes=[RNG("xout", 0, 1)], inc=16)

    S.barrier()
    p.close()
    return nc



def _ns_C():
    pass

D = 1024
T = 4096
DIN = 2048
NST = 16
NTC = 8
TC = 512


def build_C1(ntc=NTC):
    p = Prog()
    nc = p.nc
    rB = nc.dram_tensor("rB", [D, T], F32, kind="ExternalInput").ap()
    winT = nc.dram_tensor("winT", [D, 2 * DIN], F32, kind="ExternalInput").ap() if False \
        else nc.dram_tensor("winT", [D, 4096], F32, kind="ExternalInput").ap()
    wcH = nc.dram_tensor("wcH", [128, 64], F32, kind="ExternalInput").ap()
    bconvH = nc.dram_tensor("bconvH", [128, 16], F32, kind="ExternalInput").ap()
    wxT = nc.dram_tensor("wxT", [DIN, 96], F32, kind="ExternalInput").ap()
    wdtT = nc.dram_tensor("wdtT", [64, DIN], F32, kind="ExternalInput").ap()
    bdtH = nc.dram_tensor("bdtH", [128, 16], F32, kind="ExternalInput").ap()
    AlogL = nc.dram_tensor("AlogL", [128, 256], F32, kind="ExternalInput").ap()
    DpH = nc.dram_tensor("DpH", [128, 16], F32, kind="ExternalInput").ap()
    woutT = nc.dram_tensor("woutT", [DIN, D], F32, kind="ExternalInput").ap()
    wpT = nc.dram_tensor("wpT", [D, D], F32, kind="ExternalInput").ap()
    bpH = nc.dram_tensor("bpH", [128, 8], F32, kind="ExternalInput").ap()
    ind16D = nc.dram_tensor("ind16D", [128, 16 * 128], F32, kind="ExternalInput").ap()
    indBD = nc.dram_tensor("indBD", [96, 128], F32, kind="ExternalInput").ap()
    indCD = nc.dram_tensor("indCD", [96, 128], F32, kind="ExternalInput").ap()
    indRD = nc.dram_tensor("indRD", [128, 16 * 128], F32, kind="ExternalInput").ap()
    part = nc.dram_tensor("part", [D, T], F32, kind="ExternalOutput").ap()

    xcb = p.sb("xcb", [128, 8 * TC], BF16)
    wbuf = p.sb("wbuf", [128, 2 * 2048], BF16)
    upre = p.sb("upre", [128, 16 * (TC + 3)], BF16)
    ut = p.sb("ut", [128, 16 * TC], F32)
    ub = p.sb("ub", [128, 16 * TC], BF16)
    zb = p.sb("zb", [128, 16 * TC], BF16)
    delta = p.sb("delta", [128, 16 * TC], F32)
    du = p.sb("du", [128, 16 * TC], BF16)
    dAg = p.sb("dAg", [128, 2 * TC], F32)
    dBg = p.sb("dBg", [128, 2 * TC], F32)
    hg = p.sb("hg", [128, 2 * TC], F32)
    hcg = p.sb("hcg", [128, 2 * TC], BF16)
    ytc = p.sb("ytc", [128, TC], F32)
    state = p.sb("state", [128, 256], F32)
    Acol = p.sb("Acol", [128, 256], F32)
    ind16 = p.sb("ind16", [128, 16 * 128], F32)
    ind16b = p.sb("ind16b", [128, 16 * 128], BF16)
    indR = p.sb("indR", [128, 16 * 128], BF16)
    indB = p.sb("indB", [96, 128], BF16)
    indC = p.sb("indC", [96, 128], BF16)
    dbcb = p.sb("dbcb", [96, TC], BF16)
    wct = p.sb("wct", [128, 64], F32)
    bconvT = p.sb("bconvT", [128, 16], F32)
    bdtT = p.sb("bdtT", [128, 16], F32)
    DpT = p.sb("DpT", [128, 16], F32)
    bpT = p.sb("bpT", [128, 8], F32)
    onesc = p.sb("onesc", [128, 1], F32)
    ones1 = p.sb("ones1", [1, 128], F32)
    rF = p.sb("rF", [128, 8 * TC], F32)
    stg = rF
    Brep_t = rF
    Crep_t = rF
    dbc_t = rF
    sz = sq_alias = None
    sq = p.sb("sq", [128, 2 * TC], F32)

    ring = [p.ps(f"pm{i}", [128, 512], F32) for i in range(4)]
    pY = p.ps("pY", [128, 512], F32)
    pL1 = p.ps("pL1", [128, 512], F32)
    pL2 = p.ps("pL2", [128, 512], F32)

    blk = p.block()
    S = Sched(p, blk)
    RNG = lambda n, a, b: (n, a, b)
    ringR = [RNG(f"pm{i}", 0, 512) for i in range(4)]
    pYR = RNG("pY", 0, 512)

    def dmal(eng, out_ap, in_ap, wr, rd=()):
        S.emit(eng, lambda E: E.dma_start(out=out_ap, in_=in_ap), reads=rd, writes=[wr], inc=16)

    dmal("y", wct[:], wcH[:], RNG("wct", 0, 64))
    dmal("y", bconvT[:], bconvH[:], RNG("bconvT", 0, 16))
    dmal("y", bdtT[:], bdtH[:], RNG("bdtT", 0, 16))
    dmal("y", DpT[:], DpH[:], RNG("DpT", 0, 16))
    dmal("y", bpT[:], bpH[:], RNG("bpT", 0, 8))
    dmal("y", Acol[:], AlogL[:], RNG("Acol", 0, 256))
    dmal("y", ind16[:], ind16D[:], RNG("ind16", 0, 2048))
    dmal("g", ind16b[:], ind16D[:], RNG("ind16b", 0, 2048))
    dmal("g", indR[:], indRD[:], RNG("indR", 0, 2048))
    dmal("g", indB[:], indBD[:], RNG("indB", 0, 128))
    dmal("g", indC[:], indCD[:], RNG("indC", 0, 128))
    S.emit("v", lambda v: v.memset(onesc[:], 1.0), writes=[RNG("onesc", 0, 1)])
    S.emit("v", lambda v: v.memset(ones1[:], 1.0), writes=[RNG("ones1", 0, 128)])
    S.emit("v", lambda v: v.memset(state[:], 0.0), writes=[RNG("state", 0, 256)])
    S.emit("v", lambda v: v.memset(upre[:], 0.0), writes=[RNG("upre", 0, 16 * (TC + 3))])
    S.emit("s", lambda s_: s_.activation(out=Acol[:], in_=Acol[:], func=AF.Exp),
           reads=[RNG("Acol", 0, 256)], writes=[RNG("Acol", 0, 256)])
    S.emit("v", lambda v: v.tensor_scalar(out=Acol[:], in0=Acol[:], scalar1=-1.0,
                                          scalar2=None, op0=ALU.mult),
           reads=[RNG("Acol", 0, 256)], writes=[RNG("Acol", 0, 256)])
    wdt_sb = p.sb("wdt_sb", [64, DIN], BF16)
    dmal("g", wdt_sb[:], wdtT[:], RNG("wdt_sb", 0, DIN))
    wx_sb = p.sb("wx_sb", [128, 16 * 96], BF16)
    for c in range(16):
        dmal("g", wx_sb[:, c * 96:(c + 1) * 96], wxT[c * 128:(c + 1) * 128, :],
             RNG("wx_sb", c * 96, (c + 1) * 96))

    wslot = [0]

    def stream_w(dram, colbase, ncols=128, nk=8, rowstride=128):
        slot = (wslot[0] % 2) * 2048
        wslot[0] += 1
        for kc in range(nk):
            dmal("g", wbuf[:, slot + kc * 128:slot + kc * 128 + ncols],
                 dram[kc * rowstride:(kc + 1) * rowstride, colbase:colbase + ncols],
                 RNG("wbuf", slot + kc * 128, slot + kc * 128 + ncols))
        return slot

    for tc in range(ntc):
        t0 = tc * TC
        for c in range(8):
            dmal("y", rF[:, c * TC:(c + 1) * TC], rB[c * 128:(c + 1) * 128, t0:t0 + TC],
                 RNG("rF", c * TC, (c + 1) * TC))

        def s1f(t):
            r = None
            for c in range(8):
                r = t.matmul(pL1[0:1, 0:TC], onesc[:], rF[:, c * TC:(c + 1) * TC],
                             start=(c == 0), stop=(c == 7))
            return r
        S.emit("t", s1f, reads=[RNG("rF", 0, 8 * TC), RNG("onesc", 0, 1)],
               writes=[RNG("pL1", 0, TC)])
        for c in range(8):
            sl = (c % 2) * TC
            S.emit("s", lambda s_, c=c, sl=sl: s_.activation(out=sq[:, sl:sl + TC],
                                                             in_=rF[:, c * TC:(c + 1) * TC],
                                                             func=AF.Square),
                   reads=[RNG("rF", c * TC, (c + 1) * TC)], writes=[RNG("sq", sl, sl + TC)])
            S.emit("t", lambda t, c=c, sl=sl: t.matmul(pL2[0:1, 0:TC], onesc[:],
                                                       sq[:, sl:sl + TC],
                                                       start=(c == 0), stop=(c == 7)),
                   reads=[RNG("sq", sl, sl + TC)], writes=[RNG("pL2", 0, TC)])
        S.emit("v", lambda v: v.tensor_scalar(out=delta[0:1, 0:TC], in0=pL1[0:1, 0:TC],
                                              scalar1=1.0 / 1024.0, scalar2=None, op0=ALU.mult),
               reads=[RNG("pL1", 0, TC)], writes=[RNG("delta", 0, TC)], small=True)
        S.emit("v", lambda v: v.tensor_tensor(out=delta[0:1, 512:512 + TC], in0=delta[0:1, 0:TC],
                                              in1=delta[0:1, 0:TC], op=ALU.mult),
               reads=[RNG("delta", 0, TC)], writes=[RNG("delta", 512, 512 + TC)], small=True)
        S.emit("v", lambda v: v.tensor_scalar(out=delta[0:1, 1024:1024 + TC], in0=pL2[0:1, 0:TC],
                                              scalar1=1.0 / 1024.0, scalar2=None, op0=ALU.mult),
               reads=[RNG("pL2", 0, TC)], writes=[RNG("delta", 1024, 1024 + TC)], small=True)
        S.emit("v", lambda v: v.tensor_tensor(out=delta[0:1, 1024:1024 + TC],
                                              in0=delta[0:1, 1024:1024 + TC],
                                              in1=delta[0:1, 512:512 + TC], op=ALU.subtract),
               reads=[RNG("delta", 512, 1024 + TC)], writes=[RNG("delta", 1024, 1024 + TC)], small=True)
        S.emit("v", lambda v: v.tensor_scalar(out=delta[0:1, 1024:1024 + TC],
                                              in0=delta[0:1, 1024:1024 + TC],
                                              scalar1=1e-5, scalar2=None, op0=ALU.add),
               reads=[RNG("delta", 1024, 1024 + TC)], writes=[RNG("delta", 1024, 1024 + TC)], small=True)
        S.emit("s", lambda s_: s_.activation(out=delta[0:1, 1536:1536 + TC],
                                             in_=delta[0:1, 1024:1024 + TC], func=AF.Sqrt),
               reads=[RNG("delta", 1024, 1024 + TC)], writes=[RNG("delta", 1536, 1536 + TC)], small=True)
        S.emit("v", lambda v: v.reciprocal(out=delta[0:1, 1536:1536 + TC],
                                           in_=delta[0:1, 1536:1536 + TC]),
               reads=[RNG("delta", 1536, 1536 + TC)], writes=[RNG("delta", 1536, 1536 + TC)], small=True)
        S.emit("t", lambda t: t.matmul(ring[0][:, 0:TC], ones1[0:1, :], delta[0:1, 0:TC],
                                       start=True, stop=True),
               reads=[RNG("delta", 0, TC), RNG("ones1", 0, 128)], writes=[ringR[0]])
        S.emit("t", lambda t: t.matmul(ring[1][:, 0:TC], ones1[0:1, :], delta[0:1, 1536:1536 + TC],
                                       start=True, stop=True),
               reads=[RNG("delta", 1536, 1536 + TC)], writes=[ringR[1]])
        for c in range(8):
            sl = (c % 2) * TC
            S.emit("v", lambda v, c=c, sl=sl: v.tensor_tensor(
                out=sq[:, sl:sl + TC], in0=rF[:, c * TC:(c + 1) * TC], in1=ring[0][:, 0:TC],
                op=ALU.subtract),
                reads=[RNG("rF", c * TC, (c + 1) * TC), ringR[0]], writes=[RNG("sq", sl, sl + TC)])
            S.emit("v", lambda v, c=c, sl=sl: v.tensor_tensor(
                out=xcb[:, c * TC:(c + 1) * TC], in0=sq[:, sl:sl + TC], in1=ring[1][:, 0:TC],
                op=ALU.mult),
                reads=[RNG("sq", sl, sl + TC), ringR[1]], writes=[RNG("xcb", c * TC, (c + 1) * TC)])

        for jc in range(32):
            slot = stream_w(winT, jc * 128)

            bk = ring[jc % 4]
            bkr = ringR[jc % 4]

            def xzf(t, slot=slot, bk=bk):
                r = None
                for kc in range(8):
                    r = t.matmul(bk[:, 0:TC], wbuf[:, slot + kc * 128:slot + (kc + 1) * 128],
                                 xcb[:, kc * TC:(kc + 1) * TC], start=(kc == 0), stop=(kc == 7))
                return r
            S.emit("t", xzf, reads=[RNG("wbuf", slot, slot + 1024), RNG("xcb", 0, 8 * TC)],
                   writes=[bkr])
            if jc < 16:
                S.emit("v", lambda v, jc=jc, bk=bk: v.tensor_copy(
                    out=upre[:, jc * (TC + 3) + 3:jc * (TC + 3) + 3 + TC], in_=bk[:, 0:TC]),
                    reads=[bkr], writes=[RNG("upre", jc * (TC + 3) + 3, (jc + 1) * (TC + 3))])
            else:
                c = jc - 16
                S.emit("v", lambda v, c=c, bk=bk: v.tensor_copy(
                    out=zb[:, c * TC:(c + 1) * TC], in_=bk[:, 0:TC]),
                    reads=[bkr], writes=[RNG("zb", c * TC, (c + 1) * TC)])

        for c in range(16):
            base = c * (TC + 3)
            for k in range(4):
                src = upre[:, base + k:base + k + TC]
                if k == 0:
                    S.emit("v", lambda v, c=c, src=src, k=k: v.tensor_scalar(
                        out=sq[:, 0:TC], in0=src, scalar1=wct[:, c * 4 + k:c * 4 + k + 1],
                        scalar2=None, op0=ALU.mult),
                        reads=[RNG("upre", base, base + 3 + TC), RNG("wct", 0, 64)],
                        writes=[RNG("sq", 0, TC)])
                else:
                    S.emit("v", lambda v, c=c, src=src, k=k: v.scalar_tensor_tensor(
                        out=sq[:, 0:TC], in0=src, scalar=wct[:, c * 4 + k:c * 4 + k + 1],
                        in1=sq[:, 0:TC], op0=ALU.mult, op1=ALU.add),
                        reads=[RNG("upre", base, base + 3 + TC), RNG("sq", 0, TC), RNG("wct", 0, 64)],
                        writes=[RNG("sq", 0, TC)])
            S.emit("s", lambda s_, c=c: s_.activation(out=ut[:, c * TC:(c + 1) * TC],
                                                      in_=sq[:, 0:TC], func=AF.Silu,
                                                      bias=bconvT[:, c:c + 1], scale=1.0),
                   reads=[RNG("sq", 0, TC), RNG("bconvT", 0, 16)],
                   writes=[RNG("ut", c * TC, (c + 1) * TC)])
            S.emit("v", lambda v, c=c: v.tensor_copy(out=ub[:, c * TC:(c + 1) * TC],
                                                     in_=ut[:, c * TC:(c + 1) * TC]),
                   reads=[RNG("ut", c * TC, (c + 1) * TC)],
                   writes=[RNG("ub", c * TC, (c + 1) * TC)])
            S.emit("v", lambda v, c=c, base=base: v.tensor_copy(
                out=upre[:, base:base + 3], in_=upre[:, base + TC:base + TC + 3]),
                reads=[RNG("upre", base + TC, base + TC + 3)],
                writes=[RNG("upre", base, base + 3)], small=True)

        def dbcf(t):
            r = None
            for kc in range(16):
                r = t.matmul(ring[3][0:96, 0:TC], wx_sb[:, kc * 96:(kc + 1) * 96],
                             ub[:, kc * TC:(kc + 1) * TC], start=(kc == 0), stop=(kc == 15))
            return r
        S.emit("t", dbcf, reads=[RNG("wx_sb", 0, 16 * 96), RNG("ub", 0, 16 * TC)],
               writes=[ringR[3]])
        S.emit("v", lambda v: v.tensor_copy(out=dbcb[:, 0:TC], in_=ring[3][0:96, 0:TC]),
               reads=[ringR[3]], writes=[RNG("dbcb", 0, TC)])

        S.emit("t", lambda t: t.matmul(ring[2][:, 0:TC], indB[:, :], dbcb[:, 0:TC],
                                       start=True, stop=True),
               reads=[RNG("dbcb", 0, TC), RNG("indB", 0, 128)], writes=[ringR[2]])
        S.emit("v", lambda v: v.tensor_copy(out=rF[:, 2 * TC:3 * TC], in_=ring[2][:, 0:TC]),
               reads=[ringR[2]], writes=[RNG("rF", 2 * TC, 3 * TC)])
        S.emit("t", lambda t: t.matmul(ring[2][:, 0:TC], indC[:, :], dbcb[:, 0:TC],
                                       start=True, stop=True),
               reads=[RNG("dbcb", 0, TC), RNG("indC", 0, 128)], writes=[ringR[2]])
        S.emit("v", lambda v: v.tensor_copy(out=rF[:, 3 * TC:4 * TC], in_=ring[2][:, 0:TC]),
               reads=[ringR[2]], writes=[RNG("rF", 3 * TC, 4 * TC)])

        for c in range(16):
            S.emit("t", lambda t, c=c: t.matmul(
                ring[2][:, 0:TC], wdt_sb[:, c * 128:(c + 1) * 128],
                dbcb[0:64, 0:TC], start=True, stop=True),
                reads=[RNG("wdt_sb", c * 128, (c + 1) * 128), RNG("dbcb", 0, TC)],
                writes=[ringR[2]])
            S.emit("s", lambda s_, c=c: s_.activation(out=sq[:, 0:TC], in_=ring[2][:, 0:TC],
                                                      func=AF.Abs, bias=bdtT[:, c:c + 1], scale=1.0),
                   reads=[ringR[2], RNG("bdtT", 0, 16)], writes=[RNG("sq", 0, TC)])
            S.emit("s", lambda s_: s_.activation(out=sq[:, 0:TC], in_=sq[:, 0:TC],
                                                 func=AF.Exp, scale=-1.0),
                   reads=[RNG("sq", 0, TC)], writes=[RNG("sq", 0, TC)])
            S.emit("v", lambda v: v.tensor_scalar(out=sq[:, 0:TC], in0=sq[:, 0:TC],
                                                  scalar1=1.0, scalar2=None, op0=ALU.add),
                   reads=[RNG("sq", 0, TC)], writes=[RNG("sq", 0, TC)])
            S.emit("s", lambda s_: s_.activation(out=sq[:, 0:TC], in_=sq[:, 0:TC], func=AF.Ln),
                   reads=[RNG("sq", 0, TC)], writes=[RNG("sq", 0, TC)])
            S.emit("s", lambda s_, c=c: s_.activation(out=sq[:, TC:2 * TC], in_=ring[2][:, 0:TC],
                                                      func=AF.Relu, bias=bdtT[:, c:c + 1], scale=1.0),
                   reads=[ringR[2], RNG("bdtT", 0, 16)], writes=[RNG("sq", TC, 2 * TC)])
            S.emit("v", lambda v, c=c: v.tensor_tensor(out=delta[:, c * TC:(c + 1) * TC],
                                                       in0=sq[:, 0:TC], in1=sq[:, TC:2 * TC],
                                                       op=ALU.add),
                   reads=[RNG("sq", 0, 2 * TC)], writes=[RNG("delta", c * TC, (c + 1) * TC)])
            S.emit("v", lambda v, c=c: v.tensor_tensor(out=du[:, c * TC:(c + 1) * TC],
                                                       in0=delta[:, c * TC:(c + 1) * TC],
                                                       in1=ut[:, c * TC:(c + 1) * TC], op=ALU.mult),
                   reads=[RNG("delta", c * TC, (c + 1) * TC), RNG("ut", c * TC, (c + 1) * TC)],
                   writes=[RNG("du", c * TC, (c + 1) * TC)])

        for c in range(16):
            for i in range(16):
                g = c * 16 + i
                bb = (i % 2) * TC
                rA_, rB_ = ring[(2 * i) % 4], ring[(2 * i + 1) % 4]
                rAr, rBr = ringR[(2 * i) % 4], ringR[(2 * i + 1) % 4]
                S.emit("t", lambda t, c=c, i=i, rA_=rA_: t.matmul(
                    rA_[:, 0:TC], ind16[:, i * 128:(i + 1) * 128],
                    delta[:, c * TC:(c + 1) * TC], start=True, stop=True),
                    reads=[RNG("ind16", i * 128, (i + 1) * 128), RNG("delta", c * TC, (c + 1) * TC)],
                    writes=[rAr])
                S.emit("s", lambda s_, g=g, bb=bb, rA_=rA_: s_.activation(
                    out=dAg[:, bb:bb + TC], in_=rA_[:, 0:TC],
                    func=AF.Exp, scale=Acol[:, g:g + 1]),
                    reads=[rAr, RNG("Acol", 0, 256)], writes=[RNG("dAg", bb, bb + TC)])
                S.emit("t", lambda t, c=c, i=i, rB_=rB_: t.matmul(
                    rB_[:, 0:TC], ind16b[:, i * 128:(i + 1) * 128],
                    du[:, c * TC:(c + 1) * TC], start=True, stop=True),
                    reads=[RNG("ind16b", i * 128, (i + 1) * 128), RNG("du", c * TC, (c + 1) * TC)],
                    writes=[rBr])
                S.emit("v", lambda v, bb=bb, rB_=rB_: v.tensor_tensor(
                    out=dBg[:, bb:bb + TC], in0=rB_[:, 0:TC],
                    in1=rF[:, 2 * TC:3 * TC], op=ALU.mult),
                    reads=[rBr, RNG("rF", 2 * TC, 3 * TC)], writes=[RNG("dBg", bb, bb + TC)])
                S.emit("v", lambda v, g=g, bb=bb: v.tensor_tensor_scan(
                    out=hg[:, bb:bb + TC], data0=dAg[:, bb:bb + TC], data1=dBg[:, bb:bb + TC],
                    initial=state[:, g:g + 1], op0=ALU.mult, op1=ALU.add),
                    reads=[RNG("dAg", bb, bb + TC), RNG("dBg", bb, bb + TC), RNG("state", g, g + 1)],
                    writes=[RNG("hg", bb, bb + TC)])
                S.emit("v", lambda v, g=g, bb=bb: v.tensor_copy(out=state[:, g:g + 1],
                                                                in_=hg[:, bb + TC - 1:bb + TC]),
                       reads=[RNG("hg", bb, bb + TC)], writes=[RNG("state", g, g + 1)], small=True)
                S.emit("v", lambda v, bb=bb: v.tensor_tensor(out=hcg[:, bb:bb + TC],
                                                             in0=hg[:, bb:bb + TC], in1=rF[:, 3 * TC:4 * TC],
                                                             op=ALU.mult),
                       reads=[RNG("hg", bb, bb + TC), RNG("rF", 3 * TC, 4 * TC)],
                       writes=[RNG("hcg", bb, bb + TC)])
                S.emit("t", lambda t, i=i, bb=bb: t.matmul(pY[:, 0:TC],
                                                           indR[:, i * 128:(i + 1) * 128],
                                                           hcg[:, bb:bb + TC],
                                                           start=(i == 0), stop=(i == 15)),
                       reads=[RNG("indR", i * 128, (i + 1) * 128), RNG("hcg", bb, bb + TC)],
                       writes=[pYR])
            S.emit("v", lambda v, c=c: v.scalar_tensor_tensor(
                out=ytc[:], in0=ut[:, c * TC:(c + 1) * TC], scalar=DpT[:, c:c + 1],
                in1=pY[:, 0:TC], op0=ALU.mult, op1=ALU.add),
                reads=[RNG("ut", c * TC, (c + 1) * TC), pYR, RNG("DpT", 0, 16)],
                writes=[RNG("ytc", 0, TC)])
            S.emit("s", lambda s_, c=c: s_.activation(out=sq[:, 0:TC], in_=zb[:, c * TC:(c + 1) * TC],
                                                      func=AF.Silu),
                   reads=[RNG("zb", c * TC, (c + 1) * TC)], writes=[RNG("sq", 0, TC)])
            S.emit("v", lambda v, c=c: v.tensor_tensor(out=zb[:, c * TC:(c + 1) * TC],
                                                       in0=ytc[:], in1=sq[:, 0:TC], op=ALU.mult),
                   reads=[RNG("ytc", 0, TC), RNG("sq", 0, TC)],
                   writes=[RNG("zb", c * TC, (c + 1) * TC)])

        for oc in range(8):
            slot = stream_w(woutT, oc * 128, nk=16)

            def mamf(t, slot=slot):
                r = None
                for kc in range(16):
                    r = t.matmul(ring[2][:, 0:TC], wbuf[:, slot + kc * 128:slot + (kc + 1) * 128],
                                 zb[:, kc * TC:(kc + 1) * TC], start=(kc == 0), stop=(kc == 15))
                return r
            S.emit("t", mamf, reads=[RNG("wbuf", slot, slot + 16 * 128), RNG("zb", 0, 16 * TC)],
                   writes=[ringR[2]])
            S.emit("v", lambda v, oc=oc: v.tensor_copy(out=ub[:, oc * TC:(oc + 1) * TC],
                                                       in_=ring[2][:, 0:TC]),
                   reads=[ringR[2]], writes=[RNG("ub", oc * TC, (oc + 1) * TC)])

        for oc in range(8):
            slot = stream_w(wpT, oc * 128, nk=8)

            def prtf(t, slot=slot):
                r = None
                for kc in range(8):
                    r = t.matmul(ring[3][:, 0:TC], wbuf[:, slot + kc * 128:slot + (kc + 1) * 128],
                                 ub[:, kc * TC:(kc + 1) * TC], start=(kc == 0), stop=(kc == 7))
                return r
            S.emit("t", prtf, reads=[RNG("wbuf", slot, slot + 1024), RNG("ub", 0, 8 * TC)],
                   writes=[ringR[3]])
            sg = (oc % 2) * TC
            S.emit("v", lambda v, oc=oc, sg=sg: v.tensor_scalar(
                out=stg[:, sg:sg + TC], in0=ring[3][:, 0:TC], scalar1=bpT[:, oc:oc + 1],
                scalar2=None, op0=ALU.add),
                reads=[ringR[3], RNG("bpT", 0, 8)], writes=[RNG("rF", sg, sg + TC)])
            S.emit("y", lambda s, oc=oc, t0=t0, sg=sg: s.dma_start(
                out=part[oc * 128:(oc + 1) * 128, t0:t0 + TC], in_=stg[:, sg:sg + TC]),
                reads=[RNG("rF", sg, sg + TC)], writes=[RNG("part", 0, 1)], inc=16)

    S.barrier()
    p.close()
    return nc


def build_C2():
    p = Prog()
    nc = p.nc
    rBo = nc.dram_tensor("rBo", [D, 2048], F32, kind="ExternalInput").ap()
    pF = nc.dram_tensor("pF", [D, 2048], F32, kind="ExternalInput").ap()
    pB = nc.dram_tensor("pB", [D, 2048], F32, kind="ExternalInput").ap()
    rC = nc.dram_tensor("rC", [D, 2048], F32, kind="ExternalOutput").ap()

    rF = p.sb("rF", [128, 8 * TC], F32)
    pFt = p.sb("pFt", [128, 8 * TC], F32)
    pBt = p.sb("pBt", [128, 8 * TC], F32)
    sq = p.sb("sq", [128, 2 * TC], F32)
    onesc = p.sb("onesc", [128, 1], F32)
    ones1 = p.sb("ones1", [1, 128], F32)
    stg = p.sb("stg", [128, 2 * TC], F32)
    delta = p.sb("sttC2", [1, 2048], F32)
    ring = [p.ps(f"pm{i}", [128, 512], F32) for i in range(2)]
    pL1 = p.ps("pL1", [128, 512], F32)
    pL2 = p.ps("pL2", [128, 512], F32)

    blk = p.block()
    S = Sched(p, blk)
    RNG = lambda n, a, b: (n, a, b)
    ringR = [RNG(f"pm{i}", 0, 512) for i in range(2)]

    S.emit("v", lambda v: v.memset(onesc[:], 1.0), writes=[RNG("onesc", 0, 1)])
    S.emit("v", lambda v: v.memset(ones1[:], 1.0), writes=[RNG("ones1", 0, 128)])

    for tc in range(4):
        t0 = tc * TC
        for c in range(8):
            S.emit("y", lambda s, c=c, t0=t0: s.dma_start(
                out=rF[:, c * TC:(c + 1) * TC], in_=rBo[c * 128:(c + 1) * 128, t0:t0 + TC]),
                writes=[RNG("rF", c * TC, (c + 1) * TC)], inc=16)
            S.emit("y", lambda s, c=c, t0=t0: s.dma_start(
                out=pFt[:, c * TC:(c + 1) * TC], in_=pF[c * 128:(c + 1) * 128, t0:t0 + TC]),
                writes=[RNG("pFt", c * TC, (c + 1) * TC)], inc=16)
            S.emit("y", lambda s, c=c, t0=t0: s.dma_start(
                out=pBt[:, c * TC:(c + 1) * TC], in_=pB[c * 128:(c + 1) * 128, t0:t0 + TC]),
                writes=[RNG("pBt", c * TC, (c + 1) * TC)], inc=16)

        def s1f(t):
            r = None
            for c in range(8):
                r = t.matmul(pL1[0:1, 0:TC], onesc[:], rF[:, c * TC:(c + 1) * TC],
                             start=(c == 0), stop=(c == 7))
            return r
        S.emit("t", s1f, reads=[RNG("rF", 0, 8 * TC), RNG("onesc", 0, 1)],
               writes=[RNG("pL1", 0, TC)])
        for c in range(8):
            sl = (c % 2) * TC
            S.emit("s", lambda s_, c=c, sl=sl: s_.activation(out=sq[:, sl:sl + TC],
                                                             in_=rF[:, c * TC:(c + 1) * TC],
                                                             func=AF.Square),
                   reads=[RNG("rF", c * TC, (c + 1) * TC)], writes=[RNG("sq", sl, sl + TC)])
            S.emit("t", lambda t, c=c, sl=sl: t.matmul(pL2[0:1, 0:TC], onesc[:],
                                                       sq[:, sl:sl + TC],
                                                       start=(c == 0), stop=(c == 7)),
                   reads=[RNG("sq", sl, sl + TC)], writes=[RNG("pL2", 0, TC)])
        S.emit("v", lambda v: v.tensor_scalar(out=delta[0:1, 0:TC], in0=pL1[0:1, 0:TC],
                                              scalar1=1.0 / 1024.0, scalar2=None, op0=ALU.mult),
               reads=[RNG("pL1", 0, TC)], writes=[RNG("delta", 0, TC)], small=True)
        S.emit("v", lambda v: v.tensor_tensor(out=delta[0:1, 512:512 + TC], in0=delta[0:1, 0:TC],
                                              in1=delta[0:1, 0:TC], op=ALU.mult),
               reads=[RNG("delta", 0, TC)], writes=[RNG("delta", 512, 512 + TC)], small=True)
        S.emit("v", lambda v: v.tensor_scalar(out=delta[0:1, 1024:1024 + TC], in0=pL2[0:1, 0:TC],
                                              scalar1=1.0 / 1024.0, scalar2=None, op0=ALU.mult),
               reads=[RNG("pL2", 0, TC)], writes=[RNG("delta", 1024, 1024 + TC)], small=True)
        S.emit("v", lambda v: v.tensor_tensor(out=delta[0:1, 1024:1024 + TC],
                                              in0=delta[0:1, 1024:1024 + TC],
                                              in1=delta[0:1, 512:512 + TC], op=ALU.subtract),
               reads=[RNG("delta", 512, 1024 + TC)], writes=[RNG("delta", 1024, 1024 + TC)], small=True)
        S.emit("v", lambda v: v.tensor_scalar(out=delta[0:1, 1024:1024 + TC],
                                              in0=delta[0:1, 1024:1024 + TC],
                                              scalar1=1e-5, scalar2=None, op0=ALU.add),
               reads=[RNG("delta", 1024, 1024 + TC)], writes=[RNG("delta", 1024, 1024 + TC)], small=True)
        S.emit("s", lambda s_: s_.activation(out=delta[0:1, 1536:1536 + TC],
                                             in_=delta[0:1, 1024:1024 + TC], func=AF.Sqrt),
               reads=[RNG("delta", 1024, 1024 + TC)], writes=[RNG("delta", 1536, 1536 + TC)], small=True)
        S.emit("v", lambda v: v.reciprocal(out=delta[0:1, 1536:1536 + TC],
                                           in_=delta[0:1, 1536:1536 + TC]),
               reads=[RNG("delta", 1536, 1536 + TC)], writes=[RNG("delta", 1536, 1536 + TC)], small=True)
        S.emit("t", lambda t: t.matmul(ring[0][:, 0:TC], ones1[0:1, :], delta[0:1, 0:TC],
                                       start=True, stop=True),
               reads=[RNG("delta", 0, TC), RNG("ones1", 0, 128)], writes=[ringR[0]])
        S.emit("t", lambda t: t.matmul(ring[1][:, 0:TC], ones1[0:1, :], delta[0:1, 1536:1536 + TC],
                                       start=True, stop=True),
               reads=[RNG("delta", 1536, 1536 + TC)], writes=[ringR[1]])
        for c in range(8):
            sl = (c % 2) * TC
            S.emit("v", lambda v, c=c, sl=sl: v.tensor_tensor(
                out=sq[:, sl:sl + TC], in0=rF[:, c * TC:(c + 1) * TC], in1=ring[0][:, 0:TC],
                op=ALU.subtract),
                reads=[RNG("rF", c * TC, (c + 1) * TC), ringR[0]], writes=[RNG("sq", sl, sl + TC)])
            S.emit("v", lambda v, c=c, sl=sl: v.tensor_tensor(
                out=sq[:, sl:sl + TC], in0=sq[:, sl:sl + TC], in1=ring[1][:, 0:TC], op=ALU.mult),
                reads=[RNG("sq", sl, sl + TC), ringR[1]], writes=[RNG("sq", sl, sl + TC)])
            S.emit("v", lambda v, c=c, sl=sl: v.tensor_tensor(
                out=sq[:, sl:sl + TC], in0=sq[:, sl:sl + TC],
                in1=pFt[:, c * TC:(c + 1) * TC], op=ALU.add),
                reads=[RNG("sq", sl, sl + TC), RNG("pFt", c * TC, (c + 1) * TC)],
                writes=[RNG("sq", sl, sl + TC)])
            S.emit("v", lambda v, c=c, sl=sl: v.tensor_tensor(
                out=stg[:, sl:sl + TC], in0=sq[:, sl:sl + TC],
                in1=pBt[:, c * TC:(c + 1) * TC], op=ALU.add),
                reads=[RNG("sq", sl, sl + TC), RNG("pBt", c * TC, (c + 1) * TC)],
                writes=[RNG("stg", sl, sl + TC)])
            S.emit("y", lambda s, c=c, t0=t0, sl=sl: s.dma_start(
                out=rC[c * 128:(c + 1) * 128, t0:t0 + TC], in_=stg[:, sl:sl + TC]),
                reads=[RNG("stg", sl, sl + TC)], writes=[RNG("rC", 0, 1)], inc=16)

    S.barrier()
    p.close()
    return nc



def _ns_Dm():
    pass

D = 1024
TOWN = 2048
FF = 2048
TC = 512


def build_D(dbg=False):
    p = Prog()
    nc = p.nc
    rC = nc.dram_tensor("rC", [D, TOWN], F32, kind="ExternalInput").ap()
    WgD = nc.dram_tensor("WgD", [D, 8], F32, kind="ExternalInput").ap()
    Ws1D = nc.dram_tensor("Ws1D", [D, FF], F32, kind="ExternalInput").ap()
    Ws3D = nc.dram_tensor("Ws3D", [D, FF], F32, kind="ExternalInput").ap()
    Ws2D = nc.dram_tensor("Ws2D", [FF, D], F32, kind="ExternalInput").ap()
    We1D = nc.dram_tensor("We1D", [8 * D, FF], F32, kind="ExternalInput").ap()
    We3D = nc.dram_tensor("We3D", [8 * D, FF], F32, kind="ExternalInput").ap()
    We2D = nc.dram_tensor("We2D", [8 * FF, D], F32, kind="ExternalInput").ap()
    outD = nc.dram_tensor("out", [D, TOWN], F32, kind="ExternalOutput").ap()
    xDs = nc.dram_tensor("xDs", [D, TOWN], F32, kind="Internal").ap()
    if dbg:
        dG = nc.dram_tensor("dG", [128, 128], F32, kind="ExternalOutput").ap()
        dW = nc.dram_tensor("dW", [128, 128], F32, kind="ExternalOutput").ap()
        dY = nc.dram_tensor("dY", [128, 8 * TOWN], F32, kind="ExternalOutput").ap()
        dRD = nc.dram_tensor("dRD", [128, 8 * TC * 4], F32, kind="ExternalOutput").ap()
        dXS = nc.dram_tensor("dXS", [D, TOWN], F32, kind="ExternalOutput").ap()
        dX = nc.dram_tensor("dX", [128, 8 * TOWN], BF16, kind="ExternalOutput").ap()

    xD = p.sb("xD", [128, 8 * TOWN], BF16)
    hh = p.sb("hh", [128, 16 * TOWN], BF16)
    yacc = p.sb("yacc", [128, 8 * TOWN], F32)
    wB = p.sb("wB", [128, TOWN], F32)
    w13 = p.sb("w13", [128, 2 * 2048], BF16)
    w2b = p.sb("w2b", [128, 2 * 2048], BF16)
    gt = p.sb("gt", [128, 16 * 8], F32)
    wtok = p.sb("wtok", [128, 16 * 8], F32)
    m12 = p.sb("m12", [128, 8], F32)
    sA = p.sb("sA", [128, TC], BF16)
    sB = p.sb("sB", [128, TC], BF16)
    sq = p.sb("sq", [128, 2 * TC], F32)
    rF = p.sb("rF", [128, 8 * TC], F32)
    sttt = None
    onesc = p.sb("onesc", [128, 1], F32)
    ones1 = p.sb("ones1", [1, 128], F32)

    ring = [p.ps(f"pm{i}", [128, 512], F32) for i in range(4)]
    pT = p.ps("pT", [128, 512], F32)
    pL1 = p.ps("pL1", [128, 512], F32)
    pL2 = p.ps("pL2", [128, 512], F32)

    blk = p.block()
    S = Sched(p, blk)
    RNG = lambda n, a, b: (n, a, b)
    ringR = [RNG(f"pm{i}", 0, 512) for i in range(4)]
    pTR = RNG("pT", 0, 128)

    def dmal(eng, out_ap, in_ap, wr, rd=()):
        S.emit(eng, lambda E: E.dma_start(out=out_ap, in_=in_ap), reads=rd, writes=[wr], inc=16)

    S.emit("v", lambda v: v.memset(onesc[:], 1.0), writes=[RNG("onesc", 0, 1)])
    S.emit("v", lambda v: v.memset(ones1[:], 1.0), writes=[RNG("ones1", 0, 128)])

    identf = yacc[:, 16256:16384]

    def build_ident(g):
        g.memset(identf, 1.0)
        return g.affine_select(identf, identf, pattern=[[-1, 128]],
                               compare_op=ALU.is_equal, fill=0.0, base=0, channel_multiplier=1)
    S.emit("g", build_ident, writes=[RNG("yacc", 16256, 16384)])

    wg_sb = p.sb("wg_sb", [128, 8 * 8], BF16)
    for c in range(8):
        dmal("g", wg_sb[:, c * 8:(c + 1) * 8], WgD[c * 128:(c + 1) * 128, :],
             RNG("wg_sb", c * 8, (c + 1) * 8))

    def ln_entry(tc):
        t0 = tc * TC
        for c in range(8):
            dmal("y", rF[:, c * TC:(c + 1) * TC], rC[c * 128:(c + 1) * 128, t0:t0 + TC],
                 RNG("rF", c * TC, (c + 1) * TC))

        def s1f(t):
            r = None
            for c in range(8):
                r = t.matmul(pL1[0:1, 0:TC], onesc[:], rF[:, c * TC:(c + 1) * TC],
                             start=(c == 0), stop=(c == 7))
            return r
        S.emit("t", s1f, reads=[RNG("rF", 0, 8 * TC), RNG("onesc", 0, 1)],
               writes=[RNG("pL1", 0, TC)])
        for c in range(8):
            sl = (c % 2) * TC
            S.emit("s", lambda s_, c=c, sl=sl: s_.activation(out=sq[:, sl:sl + TC],
                                                             in_=rF[:, c * TC:(c + 1) * TC],
                                                             func=AF.Square),
                   reads=[RNG("rF", c * TC, (c + 1) * TC)], writes=[RNG("sq", sl, sl + TC)])
            S.emit("t", lambda t, c=c, sl=sl: t.matmul(pL2[0:1, 0:TC], onesc[:], sq[:, sl:sl + TC],
                                                       start=(c == 0), stop=(c == 7)),
                   reads=[RNG("sq", sl, sl + TC)], writes=[RNG("pL2", 0, TC)])
        S.emit("v", lambda v: v.tensor_scalar(out=wB[0:1, 0:TC], in0=pL1[0:1, 0:TC],
                                              scalar1=1.0 / 1024.0, scalar2=None, op0=ALU.mult),
               reads=[RNG("pL1", 0, TC)], writes=[RNG("wB", 0, TC)], small=True)
        S.emit("v", lambda v: v.tensor_tensor(out=wB[0:1, 512:512 + TC], in0=wB[0:1, 0:TC],
                                              in1=wB[0:1, 0:TC], op=ALU.mult),
               reads=[RNG("wB", 0, TC)], writes=[RNG("wB", 512, 512 + TC)], small=True)
        S.emit("v", lambda v: v.tensor_scalar(out=wB[0:1, 1024:1024 + TC], in0=pL2[0:1, 0:TC],
                                              scalar1=1.0 / 1024.0, scalar2=None, op0=ALU.mult),
               reads=[RNG("pL2", 0, TC)], writes=[RNG("wB", 1024, 1024 + TC)], small=True)
        S.emit("v", lambda v: v.tensor_tensor(out=wB[0:1, 1024:1024 + TC],
                                              in0=wB[0:1, 1024:1024 + TC],
                                              in1=wB[0:1, 512:512 + TC], op=ALU.subtract),
               reads=[RNG("wB", 512, 1024 + TC)], writes=[RNG("wB", 1024, 1024 + TC)], small=True)
        S.emit("v", lambda v: v.tensor_scalar(out=wB[0:1, 1024:1024 + TC],
                                              in0=wB[0:1, 1024:1024 + TC],
                                              scalar1=1e-5, scalar2=None, op0=ALU.add),
               reads=[RNG("wB", 1024, 1024 + TC)], writes=[RNG("wB", 1024, 1024 + TC)], small=True)
        S.emit("s", lambda s_: s_.activation(out=wB[0:1, 1536:1536 + TC],
                                             in_=wB[0:1, 1024:1024 + TC], func=AF.Sqrt),
               reads=[RNG("wB", 1024, 1024 + TC)], writes=[RNG("wB", 1536, 1536 + TC)], small=True)
        S.emit("v", lambda v: v.reciprocal(out=wB[0:1, 1536:1536 + TC],
                                           in_=wB[0:1, 1536:1536 + TC]),
               reads=[RNG("wB", 1536, 1536 + TC)], writes=[RNG("wB", 1536, 1536 + TC)], small=True)
        S.emit("t", lambda t: t.matmul(ring[0][:, 0:TC], ones1[0:1, :], wB[0:1, 0:TC],
                                       start=True, stop=True),
               reads=[RNG("wB", 0, TC), RNG("ones1", 0, 128)], writes=[ringR[0]])
        S.emit("t", lambda t: t.matmul(ring[1][:, 0:TC], ones1[0:1, :], wB[0:1, 1536:1536 + TC],
                                       start=True, stop=True),
               reads=[RNG("wB", 1536, 1536 + TC)], writes=[ringR[1]])
        for c in range(8):
            sl = (c % 2) * TC
            S.emit("v", lambda v, c=c, sl=sl: v.tensor_tensor(
                out=sq[:, sl:sl + TC], in0=rF[:, c * TC:(c + 1) * TC], in1=ring[0][:, 0:TC],
                op=ALU.subtract),
                reads=[RNG("rF", c * TC, (c + 1) * TC), ringR[0]], writes=[RNG("sq", sl, sl + TC)])
            S.emit("v", lambda v, c=c, sl=sl: v.tensor_tensor(
                out=sq[:, sl:sl + TC], in0=sq[:, sl:sl + TC], in1=ring[1][:, 0:TC], op=ALU.mult),
                reads=[RNG("sq", sl, sl + TC), ringR[1]], writes=[RNG("sq", sl, sl + TC)])
            S.emit("v", lambda v, c=c, sl=sl, t0=t0: v.tensor_copy(
                out=xD[:, c * TOWN + t0:c * TOWN + t0 + TC], in_=sq[:, sl:sl + TC]),
                reads=[RNG("sq", sl, sl + TC)],
                writes=[RNG("xD", c * TOWN + t0, c * TOWN + t0 + TC)])
            S.emit("y", lambda s, c=c, sl=sl, t0=t0: s.dma_start(
                out=xDs[c * 128:(c + 1) * 128, t0:t0 + TC], in_=sq[:, sl:sl + TC]),
                reads=[RNG("sq", sl, sl + TC)],
                writes=[RNG("xDs", c * 4096 + t0, c * 4096 + t0 + TC)], inc=16)

    for tc in range(4):
        ln_entry(tc)

    for tt in range(16):
        def gmm(t, tt=tt):
            r = None
            for kc in range(8):
                r = t.matmul(ring[2][:, 0:8], xD[:, kc * TOWN + tt * 128:kc * TOWN + (tt + 1) * 128],
                             wg_sb[:, kc * 8:(kc + 1) * 8], start=(kc == 0), stop=(kc == 7))
            return r
        S.emit("t", gmm, reads=[RNG("xD", 0, 8 * TOWN), RNG("wg_sb", 0, 64)], writes=[ringR[2]])
        S.emit("s", lambda s_, tt=tt: s_.activation(out=gt[:, tt * 8:(tt + 1) * 8],
                                                    in_=ring[2][:, 0:8], func=AF.Sigmoid),
               reads=[ringR[2]], writes=[RNG("gt", tt * 8, (tt + 1) * 8)])
        g8 = lambda: gt[:, tt * 8:(tt + 1) * 8]
        gr = RNG("gt", tt * 8, (tt + 1) * 8)
        S.emit("v", lambda v, tt=tt: v.reduce_max(out=m12[:, 0:1], in_=gt[:, tt * 8:(tt + 1) * 8],
                                                  axis=AX.X),
               reads=[gr], writes=[RNG("m12", 0, 1)], small=True)
        S.emit("v", lambda v, tt=tt: v.tensor_scalar(out=wtok[:, tt * 8:(tt + 1) * 8],
                                                     in0=gt[:, tt * 8:(tt + 1) * 8],
                                                     scalar1=m12[:, 0:1], scalar2=NEG,
                                                     op0=ALU.is_ge, op1=ALU.mult),
               reads=[gr, RNG("m12", 0, 1)], writes=[RNG("wtok", tt * 8, (tt + 1) * 8)], small=True)
        S.emit("v", lambda v, tt=tt: v.tensor_tensor(out=sq[:, 0:8],
                                                     in0=wtok[:, tt * 8:(tt + 1) * 8],
                                                     in1=gt[:, tt * 8:(tt + 1) * 8], op=ALU.add),
               reads=[RNG("wtok", tt * 8, (tt + 1) * 8), gr], writes=[RNG("sq", 0, 8)], small=True)
        S.emit("v", lambda v: v.reduce_max(out=m12[:, 1:2], in_=sq[:, 0:8], axis=AX.X),
               reads=[RNG("sq", 0, 8)], writes=[RNG("m12", 1, 2)], small=True)
        S.emit("v", lambda v: v.tensor_tensor(out=m12[:, 2:3], in0=m12[:, 0:1], in1=m12[:, 1:2],
                                              op=ALU.add),
               reads=[RNG("m12", 0, 2)], writes=[RNG("m12", 2, 3)], small=True)
        S.emit("v", lambda v: v.reciprocal(out=m12[:, 3:4], in_=m12[:, 2:3]),
               reads=[RNG("m12", 2, 3)], writes=[RNG("m12", 3, 4)], small=True)
        S.emit("v", lambda v, tt=tt: v.tensor_scalar(out=wtok[:, tt * 8:(tt + 1) * 8],
                                                     in0=gt[:, tt * 8:(tt + 1) * 8],
                                                     scalar1=m12[:, 1:2], scalar2=None,
                                                     op0=ALU.is_ge),
               reads=[gr, RNG("m12", 1, 2)], writes=[RNG("wtok", tt * 8, (tt + 1) * 8)], small=True)
        S.emit("v", lambda v, tt=tt: v.tensor_tensor(out=wtok[:, tt * 8:(tt + 1) * 8],
                                                     in0=wtok[:, tt * 8:(tt + 1) * 8],
                                                     in1=gt[:, tt * 8:(tt + 1) * 8], op=ALU.mult),
               reads=[RNG("wtok", tt * 8, (tt + 1) * 8), gr],
               writes=[RNG("wtok", tt * 8, (tt + 1) * 8)], small=True)
        S.emit("v", lambda v, tt=tt: v.tensor_scalar(out=wtok[:, tt * 8:(tt + 1) * 8],
                                                     in0=wtok[:, tt * 8:(tt + 1) * 8],
                                                     scalar1=m12[:, 3:4], scalar2=None,
                                                     op0=ALU.mult),
               reads=[RNG("wtok", tt * 8, (tt + 1) * 8), RNG("m12", 3, 4)],
               writes=[RNG("wtok", tt * 8, (tt + 1) * 8)], small=True)
        S.emit("t", lambda t, tt=tt: t.transpose(pT[0:8, 0:128], wtok[:, tt * 8:(tt + 1) * 8], identf),
               reads=[RNG("wtok", tt * 8, (tt + 1) * 8), RNG("yacc", 16256, 16384)], writes=[pTR])
        S.emit("v", lambda v, tt=tt: v.tensor_copy(out=rF[0:8, 2048 + tt * 128:2048 + (tt + 1) * 128],
                                                   in_=pT[0:8, 0:128]),
               reads=[pTR], writes=[RNG("rF", 2048 + tt * 128, 2048 + (tt + 1) * 128)])

    if dbg:
        S.emit("y", lambda s: s.dma_start(out=dG[:], in_=gt[:]),
               reads=[RNG("gt", 0, 128)], writes=[RNG("zdG", 0, 1)], inc=16)
        S.emit("y", lambda s: s.dma_start(out=dW[:], in_=wtok[:]),
               reads=[RNG("wtok", 0, 128)], writes=[RNG("zdW", 0, 1)], inc=16)
        S.emit("y", lambda s: s.dma_start(out=dX[:], in_=xD[:]),
               reads=[RNG("xD", 0, 8 * TOWN)], writes=[RNG("zdX", 0, 1)], inc=16)

    for e in range(9):
        routed = e < 8
        if routed:
            S.emit("y", lambda s, e=e: s.dma_start(out=rF[0:1, 0:TOWN], in_=rF[e:e + 1, 2048:4096]),
                   reads=[RNG("rF", 2048, 4096)], writes=[RNG("rF", 0, TOWN)], inc=16)
            for q in range(4):
                S.emit("t", lambda t, q=q: t.matmul(ring[3][:, 0:512], ones1[0:1, :],
                                                    rF[0:1, q * 512:(q + 1) * 512],
                                                    start=True, stop=True),
                       reads=[RNG("rF", 0, TOWN), RNG("ones1", 0, 128)], writes=[ringR[3]])
                S.emit("v", lambda v, q=q: v.tensor_copy(out=wB[:, q * 512:(q + 1) * 512],
                                                         in_=ring[3][:, 0:512]),
                       reads=[ringR[3]], writes=[RNG("wB", q * 512, (q + 1) * 512)])
        for ffc in range(16):
            slot = (ffc % 2) * 2048
            for kc in range(8):
                src1 = (We1D[e * D + kc * 128:e * D + (kc + 1) * 128, ffc * 128:(ffc + 1) * 128]
                        if routed else Ws1D[kc * 128:(kc + 1) * 128, ffc * 128:(ffc + 1) * 128])
                src3 = (We3D[e * D + kc * 128:e * D + (kc + 1) * 128, ffc * 128:(ffc + 1) * 128]
                        if routed else Ws3D[kc * 128:(kc + 1) * 128, ffc * 128:(ffc + 1) * 128])
                dmal("g", w13[:, slot + kc * 128:slot + (kc + 1) * 128], src1,
                     RNG("w13", slot + kc * 128, slot + (kc + 1) * 128))
                dmal("g", w13[:, slot + 1024 + kc * 128:slot + 1024 + (kc + 1) * 128], src3,
                     RNG("w13", slot + 1024 + kc * 128, slot + 1024 + (kc + 1) * 128))
            for tc in range(4):
                def m1f(t, slot=slot, tc=tc):
                    r = None
                    for kc in range(8):
                        r = t.matmul(ring[0][:, 0:TC],
                                     w13[:, slot + kc * 128:slot + (kc + 1) * 128],
                                     xD[:, kc * TOWN + tc * TC:kc * TOWN + (tc + 1) * TC],
                                     start=(kc == 0), stop=(kc == 7))
                    return r
                S.emit("t", m1f, reads=[RNG("w13", slot, slot + 1024), RNG("xD", 0, 8 * TOWN)],
                       writes=[ringR[0]])

                def m3f(t, slot=slot, tc=tc):
                    r = None
                    for kc in range(8):
                        r = t.matmul(ring[1][:, 0:TC],
                                     w13[:, slot + 1024 + kc * 128:slot + 1024 + (kc + 1) * 128],
                                     xD[:, kc * TOWN + tc * TC:kc * TOWN + (tc + 1) * TC],
                                     start=(kc == 0), stop=(kc == 7))
                    return r
                S.emit("t", m3f, reads=[RNG("w13", slot + 1024, slot + 2048), RNG("xD", 0, 8 * TOWN)],
                       writes=[ringR[1]])
                S.emit("s", lambda s_: s_.activation(out=sA[:], in_=ring[0][:, 0:TC], func=AF.Silu),
                       reads=[ringR[0]], writes=[RNG("sA", 0, TC)])
                S.emit("v", lambda v: v.tensor_copy(out=sB[:], in_=ring[1][:, 0:TC]),
                       reads=[ringR[1]], writes=[RNG("sB", 0, TC)])
                if routed:
                    S.emit("v", lambda v: v.tensor_tensor(out=sA[:], in0=sA[:], in1=sB[:],
                                                          op=ALU.mult),
                           reads=[RNG("sA", 0, TC), RNG("sB", 0, TC)], writes=[RNG("sA", 0, TC)])
                    S.emit("v", lambda v, ffc=ffc, tc=tc: v.tensor_tensor(
                        out=hh[:, ffc * TOWN + tc * TC:ffc * TOWN + (tc + 1) * TC],
                        in0=sA[:], in1=wB[:, tc * TC:(tc + 1) * TC], op=ALU.mult),
                        reads=[RNG("sA", 0, TC), RNG("wB", tc * TC, (tc + 1) * TC)],
                        writes=[RNG("hh", ffc * TOWN + tc * TC, ffc * TOWN + (tc + 1) * TC)])
                else:
                    S.emit("v", lambda v, ffc=ffc, tc=tc: v.tensor_tensor(
                        out=hh[:, ffc * TOWN + tc * TC:ffc * TOWN + (tc + 1) * TC],
                        in0=sA[:], in1=sB[:], op=ALU.mult),
                        reads=[RNG("sA", 0, TC), RNG("sB", 0, TC)],
                        writes=[RNG("hh", ffc * TOWN + tc * TC, ffc * TOWN + (tc + 1) * TC)])
        for do in range(8):
            slot2 = (do % 2) * 2048
            for kc in range(16):
                src2 = (We2D[e * FF + kc * 128:e * FF + (kc + 1) * 128, do * 128:(do + 1) * 128]
                        if routed else Ws2D[kc * 128:(kc + 1) * 128, do * 128:(do + 1) * 128])
                dmal("g", w2b[:, slot2 + kc * 128:slot2 + (kc + 1) * 128], src2,
                     RNG("w2b", slot2 + kc * 128, slot2 + (kc + 1) * 128))
            for tc in range(4):
                def y2f(t, slot2=slot2, tc=tc):
                    r = None
                    for kc in range(16):
                        r = t.matmul(ring[2][:, 0:TC],
                                     w2b[:, slot2 + kc * 128:slot2 + (kc + 1) * 128],
                                     hh[:, kc * TOWN + tc * TC:kc * TOWN + (tc + 1) * TC],
                                     start=(kc == 0), stop=(kc == 15))
                    return r
                S.emit("t", y2f, reads=[RNG("w2b", slot2, slot2 + 2048), RNG("hh", 0, 16 * TOWN)],
                       writes=[ringR[2]])
                dst = RNG("yacc", do * TOWN + tc * TC, do * TOWN + (tc + 1) * TC)
                if e == 0:
                    S.emit("v", lambda v, do=do, tc=tc: v.tensor_copy(
                        out=yacc[:, do * TOWN + tc * TC:do * TOWN + (tc + 1) * TC],
                        in_=ring[2][:, 0:TC]),
                        reads=[ringR[2]], writes=[dst])
                else:
                    S.emit("v", lambda v, do=do, tc=tc: v.tensor_tensor(
                        out=yacc[:, do * TOWN + tc * TC:do * TOWN + (tc + 1) * TC],
                        in0=yacc[:, do * TOWN + tc * TC:do * TOWN + (tc + 1) * TC],
                        in1=ring[2][:, 0:TC], op=ALU.add),
                        reads=[ringR[2], dst], writes=[dst])

    if dbg:
        S.emit("y", lambda s: s.dma_start(out=dY[:], in_=yacc[:]),
               reads=[RNG("yacc", 0, 8 * TOWN)], writes=[RNG("zdY", 0, 1)], inc=16)

    if dbg:
        for c in range(8):
            S.emit("y", lambda s, c=c: s.dma_start(
                out=dXS[c * 128:(c + 1) * 128, :], in_=xDs[c * 128:(c + 1) * 128, :]),
                reads=[RNG("xDs", c * 4096, c * 4096 + 2048)], writes=[RNG("zdXS", 0, 1)], inc=16)
    S.barrier()
    for tc in range(4):
        S.barrier()
        t0 = tc * TC
        for c in range(8):
            dmal("y", rF[:, c * TC:(c + 1) * TC], xDs[c * 128:(c + 1) * 128, t0:t0 + TC],
                 RNG("rF", c * TC, (c + 1) * TC),
                 rd=[RNG("xDs", c * 4096 + t0, c * 4096 + t0 + TC)])
        for c in range(8):
            S.emit("v", lambda v, c=c, t0=t0: v.scalar_tensor_tensor(
                out=rF[:, c * TC:(c + 1) * TC],
                in0=yacc[:, c * TOWN + t0:c * TOWN + t0 + TC], scalar=1.0 / 3.0,
                in1=rF[:, c * TC:(c + 1) * TC], op0=ALU.mult, op1=ALU.add),
                reads=[RNG("yacc", c * TOWN + t0, c * TOWN + t0 + TC),
                       RNG("rF", c * TC, (c + 1) * TC)],
                writes=[RNG("rF", c * TC, (c + 1) * TC)])

        if dbg:
            for c in range(8):
                S.emit("y", lambda s, c=c, tc=tc: s.dma_start(
                    out=dRD[:, tc * 8 * TC + c * TC:tc * 8 * TC + (c + 1) * TC],
                    in_=rF[:, c * TC:(c + 1) * TC]),
                    reads=[RNG("rF", c * TC, (c + 1) * TC)], writes=[RNG("zdRD", 0, 1)], inc=16)

        def s1f(t):
            r = None
            for c in range(8):
                r = t.matmul(pL1[0:1, 0:TC], onesc[:], rF[:, c * TC:(c + 1) * TC],
                             start=(c == 0), stop=(c == 7))
            return r
        S.emit("t", s1f, reads=[RNG("rF", 0, 8 * TC), RNG("onesc", 0, 1)],
               writes=[RNG("pL1", 0, TC)])
        for c in range(8):
            sl = (c % 2) * TC
            S.emit("s", lambda s_, c=c, sl=sl: s_.activation(out=sq[:, sl:sl + TC],
                                                             in_=rF[:, c * TC:(c + 1) * TC],
                                                             func=AF.Square),
                   reads=[RNG("rF", c * TC, (c + 1) * TC)], writes=[RNG("sq", sl, sl + TC)])
            S.emit("t", lambda t, c=c, sl=sl: t.matmul(pL2[0:1, 0:TC], onesc[:], sq[:, sl:sl + TC],
                                                       start=(c == 0), stop=(c == 7)),
                   reads=[RNG("sq", sl, sl + TC)], writes=[RNG("pL2", 0, TC)])
        S.emit("v", lambda v: v.tensor_scalar(out=wB[0:1, 0:TC], in0=pL1[0:1, 0:TC],
                                              scalar1=1.0 / 1024.0, scalar2=None, op0=ALU.mult),
               reads=[RNG("pL1", 0, TC)], writes=[RNG("wB", 0, TC)], small=True)
        S.emit("v", lambda v: v.tensor_tensor(out=wB[0:1, 512:512 + TC], in0=wB[0:1, 0:TC],
                                              in1=wB[0:1, 0:TC], op=ALU.mult),
               reads=[RNG("wB", 0, TC)], writes=[RNG("wB", 512, 512 + TC)], small=True)
        S.emit("v", lambda v: v.tensor_scalar(out=wB[0:1, 1024:1024 + TC], in0=pL2[0:1, 0:TC],
                                              scalar1=1.0 / 1024.0, scalar2=None, op0=ALU.mult),
               reads=[RNG("pL2", 0, TC)], writes=[RNG("wB", 1024, 1024 + TC)], small=True)
        S.emit("v", lambda v: v.tensor_tensor(out=wB[0:1, 1024:1024 + TC],
                                              in0=wB[0:1, 1024:1024 + TC],
                                              in1=wB[0:1, 512:512 + TC], op=ALU.subtract),
               reads=[RNG("wB", 512, 1024 + TC)], writes=[RNG("wB", 1024, 1024 + TC)], small=True)
        S.emit("v", lambda v: v.tensor_scalar(out=wB[0:1, 1024:1024 + TC],
                                              in0=wB[0:1, 1024:1024 + TC],
                                              scalar1=1e-5, scalar2=None, op0=ALU.add),
               reads=[RNG("wB", 1024, 1024 + TC)], writes=[RNG("wB", 1024, 1024 + TC)], small=True)
        S.emit("s", lambda s_: s_.activation(out=wB[0:1, 1536:1536 + TC],
                                             in_=wB[0:1, 1024:1024 + TC], func=AF.Sqrt),
               reads=[RNG("wB", 1024, 1024 + TC)], writes=[RNG("wB", 1536, 1536 + TC)], small=True)
        S.emit("v", lambda v: v.reciprocal(out=wB[0:1, 1536:1536 + TC],
                                           in_=wB[0:1, 1536:1536 + TC]),
               reads=[RNG("wB", 1536, 1536 + TC)], writes=[RNG("wB", 1536, 1536 + TC)], small=True)
        S.emit("t", lambda t: t.matmul(ring[0][:, 0:TC], ones1[0:1, :], wB[0:1, 0:TC],
                                       start=True, stop=True),
               reads=[RNG("wB", 0, TC), RNG("ones1", 0, 128)], writes=[ringR[0]])
        S.emit("t", lambda t: t.matmul(ring[1][:, 0:TC], ones1[0:1, :], wB[0:1, 1536:1536 + TC],
                                       start=True, stop=True),
               reads=[RNG("wB", 1536, 1536 + TC)], writes=[ringR[1]])
        S.emit("v", lambda v, t0=t0: v.tensor_copy(out=yacc[:, t0:t0 + TC], in_=ring[0][:, 0:TC]),
               reads=[ringR[0]], writes=[RNG("yacc", t0, t0 + TC)])
        S.emit("v", lambda v, t0=t0: v.tensor_copy(out=yacc[:, TOWN + t0:TOWN + t0 + TC],
                                                   in_=ring[1][:, 0:TC]),
               reads=[ringR[1]], writes=[RNG("yacc", TOWN + t0, TOWN + t0 + TC)])
        for c in range(8):
            sl = (c % 2) * TC
            S.emit("v", lambda v, c=c, sl=sl, t0=t0: v.tensor_tensor(
                out=sq[:, sl:sl + TC], in0=rF[:, c * TC:(c + 1) * TC], in1=yacc[:, t0:t0 + TC],
                op=ALU.subtract),
                reads=[RNG("rF", c * TC, (c + 1) * TC), RNG("yacc", t0, t0 + TC)],
                writes=[RNG("sq", sl, sl + TC)])
            S.emit("v", lambda v, c=c, sl=sl, t0=t0: v.tensor_tensor(
                out=sq[:, sl:sl + TC], in0=sq[:, sl:sl + TC],
                in1=yacc[:, TOWN + t0:TOWN + t0 + TC], op=ALU.mult),
                reads=[RNG("sq", sl, sl + TC), RNG("yacc", TOWN + t0, TOWN + t0 + TC)],
                writes=[RNG("sq", sl, sl + TC)])
            S.emit("y", lambda s, c=c, t0=t0, sl=sl: s.dma_start(
                out=outD[c * 128:(c + 1) * 128, t0:t0 + TC], in_=sq[:, sl:sl + TC]),
                reads=[RNG("sq", sl, sl + TC)], writes=[RNG("out", 0, 1)], inc=16)

    S.barrier()
    p.close()
    return nc



NEGM = np.float32(-1.0e30)


def rotary_maps(pos, qk_scale_fold=None):
    DIM_HEAD, WINDOW = 64, 512
    half = np.arange(0, DIM_HEAD, 2, dtype=np.float64)
    inv_freq = 1.0 / (10000.0 ** (half / DIM_HEAD))
    t = pos.astype(np.float64)
    freqs = t[:, None] * inv_freq[None, :]
    freqs = np.concatenate([freqs, freqs], axis=1)
    base = (half + 0.4 * DIM_HEAD) / (1.4 * DIM_HEAD)
    power = (t - 4096 // 2) / float(WINDOW // 2)
    scale = base[None, :] ** power[:, None]
    scale = np.concatenate([scale, scale], axis=1)
    cos, sin = np.cos(freqs), np.sin(freqs)
    if qk_scale_fold is not None:
        CS = (cos * scale * qk_scale_fold).T
        SS = (sin * scale * qk_scale_fold).T
    else:
        CS = (cos / scale).T
        SS = (sin / scale).T
    CS = np.vstack([CS, CS]).astype(np.float32)
    SS = np.vstack([SS, SS]).astype(np.float32)
    return np.ascontiguousarray(CS), np.ascontiguousarray(SS)


def mloc_mask():
    M = np.full((128, 4096), NEGM, np.float32)
    for icl in range(4):
        li = icl * 128 + np.arange(128)[:, None]
        jr = np.arange(1024)[None, :]
        keep = (jr >= li) & (jr <= li + 512)
        M[:, icl * 1024:(icl + 1) * 1024] = np.where(keep, 0.0, NEGM)
    return M


def rot_RT():
    R = np.zeros((128, 128), np.float32)
    for b in (0, 64):
        for i in range(32):
            R[b + i, b + i + 32] = -1.0
            R[b + i + 32, b + i] = 1.0
    return np.ascontiguousarray(R.T)


def grp_mats():
    grpT = np.zeros((2, 128), np.float32)
    grpT[0, :64] = 1.0
    grpT[1, 64:] = 1.0
    return grpT, np.ascontiguousarray(grpT.T)


def colsplit(v, width=128):
    n = v.shape[0]
    return np.ascontiguousarray(v.reshape(n // width, width).T.astype(np.float32))


def pad_ffn(w1, b1, w2):
    HID, HID2 = 2730, 2816
    a, g = w1[:, :HID], w1[:, HID:]
    ba, bg = b1[:HID], b1[HID:]
    ap = np.zeros((1024, HID2), np.float32); ap[:, :HID] = a
    gp = np.zeros((1024, HID2), np.float32); gp[:, :HID] = g
    w1p = np.ascontiguousarray(np.concatenate([ap, gp], axis=1))
    b1ap = np.zeros(HID2, np.float32); b1ap[:HID] = ba
    b1gp = np.zeros(HID2, np.float32); b1gp[:HID] = bg
    w2p = np.zeros((HID2, 1024), np.float32); w2p[:HID] = w2
    return w1p, colsplit(b1ap), colsplit(b1gp), np.ascontiguousarray(w2p)


def b_core_inputs(core, layer_idx, rin_full_batch, inp, x0s=None):
    b, h = core // 2, core % 2
    start = h * 2048
    HALO = 1088
    m = {}
    xin = np.zeros((1024, HALO + 2048), np.float32)
    lo = start - HALO
    src_lo = max(lo, 0)
    xin[:, src_lo - lo:] = rin_full_batch[:, src_lo:start + 2048]
    m["rin"] = np.ascontiguousarray(xin)
    l = layer_idx
    m["wqkv"] = np.ascontiguousarray(np.asarray(inp["L0_ffn_wqkv"])[l])
    qs = np.asarray(inp["L0_ffn_q_scale"])[l]
    ks = np.asarray(inp["L0_ffn_k_scale"])[l]
    m["qsh"] = np.ascontiguousarray(np.concatenate([qs, qs]).astype(np.float32).reshape(128, 1))
    m["ksh"] = np.ascontiguousarray(np.concatenate([ks, ks]).astype(np.float32).reshape(128, 1))
    m["woD"] = np.ascontiguousarray(np.asarray(inp["L0_ffn_wo"])[l])
    m["lngh"] = colsplit(np.asarray(inp["L0_ffn_ln_g"])[l])
    m["lnbh"] = colsplit(np.asarray(inp["L0_ffn_ln_b"])[l])
    w1p, b1a, b1g, w2p = pad_ffn(np.asarray(inp["L0_ffn_w1"])[l],
                                 np.asarray(inp["L0_ffn_b1"])[l],
                                 np.asarray(inp["L0_ffn_w2"])[l])
    m["w1D"], m["b1aD"], m["b1gD"], m["w2D"] = w1p, b1a, b1g, w2p
    m["b2D"] = colsplit(np.asarray(inp["L0_ffn_b2"])[l])
    qpos = np.arange(start - 128, start + 2048)
    kpos = np.arange(start - 1024, start + 2048)
    m["CSqD"], m["SSqD"] = rotary_maps(qpos, qk_scale_fold=8.0)
    m["CSkD"], m["SSkD"] = rotary_maps(kpos, None)
    m["MlocD"] = mloc_mask()
    m["pad1D"] = np.full((128, 512), NEGM if h == 0 else 0.0, np.float32)
    m["RTbD"] = rot_RT()
    grpT, g64 = grp_mats()
    m["grpTD"], m["g64D"] = grpT, g64
    if x0s is not None:
        m["x0si"] = x0s
    return m


def c1_core_inputs(core, r2_full_batch, inp):
    b, d = core // 2, core % 2
    sfx = "f" if d == 0 else "b"
    m = {}
    x = r2_full_batch
    if d == 1:
        x = x[:, ::-1]
    m["rB"] = np.ascontiguousarray(x)
    win = np.asarray(inp[f"L1_attn_win_{sfx}"])
    m["winT"] = np.ascontiguousarray(win.T)
    wconv = np.asarray(inp[f"L1_attn_wconv_{sfx}"])
    m["wcH"] = np.ascontiguousarray(wconv.reshape(16, 128, 4).transpose(1, 0, 2).reshape(128, 64))
    m["bconvH"] = colsplit(np.asarray(inp[f"L1_attn_bconv_{sfx}"]))
    m["wxT"] = np.ascontiguousarray(np.asarray(inp[f"L1_attn_wx_{sfx}"]).T)
    m["wdtT"] = np.ascontiguousarray(np.asarray(inp[f"L1_attn_wdt_{sfx}"]).T)
    m["bdtH"] = colsplit(np.asarray(inp[f"L1_attn_bdt_{sfx}"]))
    Alog = np.asarray(inp[f"L1_attn_Alog_{sfx}"])
    m["AlogL"] = np.ascontiguousarray(Alog.reshape(256, 8 * 16).T.astype(np.float32))
    m["DpH"] = colsplit(np.asarray(inp[f"L1_attn_Dp_{sfx}"]))
    m["woutT"] = np.ascontiguousarray(np.asarray(inp[f"L1_attn_wout_{sfx}"]).T)
    wproj = np.asarray(inp["L1_attn_wproj"])
    m["wpT"] = np.ascontiguousarray(wproj[:, d * 1024:(d + 1) * 1024].T)
    bp = np.asarray(inp["L1_attn_bproj"]) if d == 0 else np.zeros(1024, np.float32)
    m["bpH"] = colsplit(bp)
    I16 = np.zeros((128, 16, 128), np.float32)
    for i in range(16):
        for mm_ in range(128):
            I16[i * 8 + mm_ // 16, i, mm_] = 1.0
    m["ind16D"] = np.ascontiguousarray(I16.reshape(128, 16 * 128))
    IR = np.zeros((128, 16, 128), np.float32)
    for i in range(16):
        for pp in range(128):
            IR[pp, i, i * 8 + pp // 16] = 1.0
    m["indRD"] = np.ascontiguousarray(IR.reshape(128, 16 * 128))
    IB = np.zeros((96, 128), np.float32)
    IC = np.zeros((96, 128), np.float32)
    for pp in range(128):
        IB[64 + pp % 16, pp] = 1.0
        IC[80 + pp % 16, pp] = 1.0
    m["indBD"] = IB
    m["indCD"] = IC
    return m


def _run(nc, in_maps):
    return run_bass_kernel_spmd(nc, in_maps, core_ids=list(range(8)))


def kernel(**inp):
    x = np.asarray(inp["L0_attn_y_all"])
    Wq = np.asarray(inp["L0_attn_Wq"]); bq = np.asarray(inp["L0_attn_bq"])
    Wk = np.asarray(inp["L0_attn_Wk"]); bk = np.asarray(inp["L0_attn_bk"])
    Wv = np.asarray(inp["L0_attn_Wv"]); bv = np.asarray(inp["L0_attn_bv"])
    Wo = np.asarray(inp["L0_attn_Wo"]); bo = np.asarray(inp["L0_attn_bo"])
    xfm = [np.ascontiguousarray(x[b].T) for b in range(4)]

    nc = build_A1()
    im = [{"xo": np.ascontiguousarray(xfm[c // 2][:, (c % 2) * 2048:(c % 2 + 1) * 2048]),
           "Wq": Wq, "Wk": Wk, "Wv": Wv,
           "bq": np.ascontiguousarray(bq.reshape(8, 128).T),
           "bk": np.ascontiguousarray(bk.reshape(8, 128).T),
           "bv": bv.reshape(1, 1024)} for c in range(8)]
    r1 = _run(nc, im)

    gix = np.zeros((8, 128, 16), np.float32)
    for c in range(8):
        for icn in range(16):
            gix[c, :, icn] = (c % 2) * 2048 + icn * 128 + np.arange(128)
    nc = build_A2()
    im = []
    for c in range(8):
        b = c // 2
        im.append({
            "qf": r1.results[c]["qo"],
            "kF": np.ascontiguousarray(np.concatenate(
                [r1.results[2 * b]["ko"], r1.results[2 * b + 1]["ko"]], axis=1)),
            "vTF": np.ascontiguousarray(np.concatenate(
                [r1.results[2 * b]["vTo"], r1.results[2 * b + 1]["vTo"]], axis=0)),
            "Wo": Wo, "bo": np.ascontiguousarray(bo.reshape(8, 128).T),
            "xo": np.ascontiguousarray(xfm[b][:, (c % 2) * 2048:(c % 2 + 1) * 2048]),
            "gix": gix[c]})
    r2 = _run(nc, im)
    rA = [np.concatenate([r2.results[2 * b]["r"], r2.results[2 * b + 1]["r"]], axis=1)
          for b in range(4)]

    nc = build_B(False)
    im = [b_core_inputs(c, 0, rA[c // 2], inp) for c in range(8)]
    rb1 = _run(nc, im)
    x1d = [np.zeros((1024, 4096), np.float32) for _ in range(4)]
    for c in range(8):
        b, h = c // 2, c % 2
        x1d[b][:, h * 2048:(h + 1) * 2048] = rb1.results[c]["xout"]
    nc = build_B(True)
    im = [b_core_inputs(c, 1, x1d[c // 2], inp, x0s=rb1.results[c]["x0so"]) for c in range(8)]
    rb2 = _run(nc, im)
    r2d = [np.zeros((1024, 4096), np.float32) for _ in range(4)]
    for c in range(8):
        b, h = c // 2, c % 2
        r2d[b][:, h * 2048:(h + 1) * 2048] = rb2.results[c]["xout"]

    nc = build_C1()
    im = [c1_core_inputs(c, r2d[c // 2], inp) for c in range(8)]
    rc1 = _run(nc, im)
    parts = []
    for c in range(8):
        pt = rc1.results[c]["part"]
        if c % 2 == 1:
            pt = pt[:, ::-1]
        parts.append(pt)
    nc = build_C2()
    im = []
    for c in range(8):
        b, h = c // 2, c % 2
        sl = slice(h * 2048, (h + 1) * 2048)
        im.append({"rBo": np.ascontiguousarray(r2d[b][:, sl]),
                   "pF": np.ascontiguousarray(parts[2 * b][:, sl]),
                   "pB": np.ascontiguousarray(parts[2 * b + 1][:, sl])})
    rc2 = _run(nc, im)

    nc = build_D()
    im = []
    for c in range(8):
        im.append({
            "rC": rc2.results[c]["rC"],
            "WgD": np.asarray(inp["L1_ffn_Wg"]),
            "Ws1D": np.asarray(inp["L1_ffn_Ws1"]),
            "Ws3D": np.asarray(inp["L1_ffn_Ws3"]),
            "Ws2D": np.asarray(inp["L1_ffn_Ws2"]),
            "We1D": np.ascontiguousarray(np.asarray(inp["L1_ffn_We1"]).reshape(8 * 1024, 2048)),
            "We3D": np.ascontiguousarray(np.asarray(inp["L1_ffn_We3"]).reshape(8 * 1024, 2048)),
            "We2D": np.ascontiguousarray(np.asarray(inp["L1_ffn_We2"]).reshape(8 * 2048, 1024)),
        })
    rd = _run(nc, im)

    out = np.zeros((4, 4096, 1024), np.float32)
    for c in range(8):
        b, h = c // 2, c % 2
        out[b, h * 2048:(h + 1) * 2048] = rd.results[c]["out"].T
    return out
